# version 27
# baseline (speedup 1.0000x reference)
import math
import numpy as np
import ml_dtypes
import concourse.bass as bass
import concourse.mybir as mybir
from concourse.bass_utils import run_bass_kernel_spmd
from concourse.alu_op_type import AluOpType
from contextlib import ExitStack

F32 = mybir.dt.float32
BF16 = mybir.dt.bfloat16
FP8 = mybir.dt.float8e4
U8 = mybir.dt.uint8
AF = mybir.ActivationFunctionType

B, L, D, H, DH = 4, 4096, 1024, 16, 64
HALF = L // 2
NT_FULL = L // 128
NT_OWN = HALF // 128
NQC = HALF // 512
EPS = 1e-5


def _slopes():
    start = 2 ** (-(2 ** -(math.log2(H) - 3)))
    return [np.float32(start ** i) for i in range(H)]


SLOPES = _slopes()

DIST_CUT = 45.0


def band_kts(hp, qc):
    W = qc * 8
    window = list(range(W, W + 8))
    s = min(float(SLOPES[2 * hp]), float(SLOPES[2 * hp + 1]))
    dmax = DIST_CUT / s if DIST_CUT is not None else 1e9
    lt, ge = [], []
    for kt in range(NT_FULL):
        if W <= kt < W + 8:
            continue
        k_lo, k_hi = kt * 128, kt * 128 + 127
        ok = False
        for half in range(2):
            q_lo = (2 * qc + half) * 512
            q_hi = q_lo + 511
            if max(q_lo - k_hi, k_lo - q_hi, 0) <= dmax:
                ok = True
        if ok:
            (lt if kt < W else ge).append(kt)
    return window, lt, ge


def _bf16_split3(v):
    out = []
    rem = v.copy()
    for _ in range(3):
        t = rem.astype(ml_dtypes.bfloat16)
        out.append(t)
        rem = rem - t.astype(np.float64)
    return out


class Rec:
    ENGS = ("sync", "gpsimd", "tensor", "vector", "scalar")

    def __init__(self, nc):
        self.nc = nc
        self.q = {e: [] for e in self.ENGS}
        self.cnt = {}

    def sem(self, handle):
        self.cnt[id(handle)] = 0
        return handle

    def val(self, sem):
        return self.cnt[id(sem)]

    def op(self, eng, meth, *a, **k):
        self.q[eng].append((meth, a, k, None, 0))

    def opi(self, eng, sem, n, meth, *a, **k):
        self.cnt[id(sem)] += n
        self.q[eng].append((meth, a, k, sem, n))

    def tag(self, eng, sem, n):
        meth, a, k, s0, n0 = self.q[eng][-1]
        assert s0 is None, "last op already tagged"
        self.cnt[id(sem)] += n
        self.q[eng][-1] = (meth, a, k, sem, n)

    def wait(self, eng, sem, v=None):
        if v is None:
            v = self.cnt[id(sem)]
        self.q[eng].append(("wait_ge", (sem, v), {}, None, 0))

    def barrier(self, sems):
        vals = [(s, self.cnt[id(s)]) for s in sems]
        for eng in self.ENGS:
            for s, v in vals:
                self.q[eng].append(("wait_ge", (s, v), {}, None, 0))

    def drain(self, eng):
        self.q[eng].append(("drain", (), {}, None, 0))

    def flush(self, block):
        for ename in self.ENGS:
            ops = self.q[ename]
            if not ops:
                continue
            dec = getattr(block, ename)

            def make(ops_l):
                def body(e):
                    for meth, a, k, sem, n in ops_l:
                        r = getattr(e, meth)(*a, **k)
                        if sem is not None:
                            r.then_inc(sem, n)
                return body
            dec(make(list(ops)))
            self.q[ename] = []


def build_nc():
    nc = bass.Bass("TRN2", target_bir_lowering=False)

    xf = nc.dram_tensor("xf", [L, D], F32, kind="ExternalInput").ap()
    xo = nc.dram_tensor("xo", [HALF, D], F32, kind="ExternalInput").ap()
    wqkv = nc.dram_tensor("wqkv", [D, 3 * D], F32, kind="ExternalInput").ap()
    bkq = nc.dram_tensor("bkq", [128, 16], F32, kind="ExternalInput").ap()
    bv = nc.dram_tensor("bv", [1, D], F32, kind="ExternalInput").ap()
    wout = nc.dram_tensor("wout", [D, D], F32, kind="ExternalInput").ap()
    bout = nc.dram_tensor("bout", [1, D], F32, kind="ExternalInput").ap()
    wfT = nc.dram_tensor("wfT", [D, D], F32, kind="ExternalInput").ap()
    bffn = nc.dram_tensor("bffn", [1, D], F32, kind="ExternalInput").ap()
    scl = nc.dram_tensor("scl", [1, 2], F32, kind="ExternalInput").ap()
    dcore = nc.dram_tensor("dcore", [128, 1408], F32, kind="ExternalInput").ap()
    u48 = nc.dram_tensor("u48", [48, 4096], BF16, kind="ExternalInput").ap()
    qrt = nc.dram_tensor("qrt", [96, 2048], BF16, kind="ExternalInput").ap()
    ones_sgn = nc.dram_tensor("ones_sgn", [6, 3072], BF16, kind="ExternalInput").ap()
    idn = nc.dram_tensor("idn", [128, 128], BF16, kind="ExternalInput").ap()
    yout = nc.dram_tensor("yout", [HALF, D], F32, kind="ExternalOutput").ap()
    qt_hbm = nc.dram_tensor("qt_hbm", [128, 8 * HALF], BF16, kind="Internal").ap()
    at_hbm = nc.dram_tensor("at_hbm", [128, 8 * HALF], BF16, kind="Internal").ap()

    es = ExitStack()
    C = es.enter_context
    with es:
        ktb = C(nc.sbuf_tensor("ktb", [128, 8 * L], BF16))
        vbuf = C(nc.sbuf_tensor("vbuf", [128, NT_FULL * (H * 65)], BF16))
        idb = C(nc.sbuf_tensor("idb", [128, 128], BF16))
        bkqs = C(nc.sbuf_tensor("bkqs", [128, 16], F32))
        scls = C(nc.sbuf_tensor("scls", [1, 2], F32))
        onesb2 = C(nc.sbuf_tensor("onesb2", [2, 128], BF16))
        bv2 = C(nc.sbuf_tensor("bv2", [2, D], BF16))
        bout2 = C(nc.sbuf_tensor("bout2", [2, D], BF16))
        bffn2 = C(nc.sbuf_tensor("bffn2", [2, D], BF16))

        scb = C(nc.sbuf_tensor("scb", [128, 4], F32))
        sc_tmp = C(nc.sbuf_tensor("sc_tmp", [1, 4], F32))
        epsc = C(nc.sbuf_tensor("epsc", [128, 1], F32))

        psA = C(nc.psum_tensor("psA", [128, 1024], F32))
        psB = C(nc.psum_tensor("psB", [128, 1024], F32))
        pvtA = C(nc.psum_tensor("pvtA", [128, 512], F32))
        pvtB = C(nc.psum_tensor("pvtB", [128, 512], F32))
        pvtAb = pvtA[:].bitcast(BF16)
        pvtBb = pvtB[:].bitcast(BF16)
        psC = C(nc.psum_tensor("psC", [128, 1024], F32))
        psCb = psC[:].bitcast(BF16)

        stn = C(nc.semaphore("stn"))
        svc = C(nc.semaphore("svc"))
        ssc = C(nc.semaphore("ssc"))
        sgp = C(nc.semaphore("sgp"))
        s_const = C(nc.semaphore("s_const"))
        s_dcs = C(nc.semaphore("s_dcs"))
        s_xt0 = C(nc.semaphore("s_xt0"))
        s_xt1 = C(nc.semaphore("s_xt1"))
        s_xt2 = C(nc.semaphore("s_xt2"))
        s_xt3 = C(nc.semaphore("s_xt3"))
        s_at1 = C(nc.semaphore("s_at1"))
        s_qsl1 = C(nc.semaphore("s_qsl1"))
        s_kx = C(nc.semaphore("s_kx"))
        s_kx2 = C(nc.semaphore("s_kx2"))
        s_xt4 = C(nc.semaphore("s_xt4"))
        s_xt5 = C(nc.semaphore("s_xt5"))
        s_xt6 = C(nc.semaphore("s_xt6"))
        s_xt7 = C(nc.semaphore("s_xt7"))
        s_qst1 = C(nc.semaphore("s_qst1"))
        s_wkv = C(nc.semaphore("s_wkv"))
        s_qst = C(nc.semaphore("s_qst"))
        s_qsl = C(nc.semaphore("s_qsl"))
        s_att = C(nc.semaphore("s_att"))
        s_atile = C(nc.semaphore("s_atile"))
        s_xot0 = C(nc.semaphore("s_xot0"))
        s_xot1 = C(nc.semaphore("s_xot1"))
        s_wtmp = C(nc.semaphore("s_wtmp"))
        s_wout = C(nc.semaphore("s_wout"))
        s_out0 = C(nc.semaphore("s_out0"))
        s_out1 = C(nc.semaphore("s_out1"))
        s_bias = C(nc.semaphore("s_bias"))
        s_att1 = C(nc.semaphore("s_att1"))
        s_wkv2 = C(nc.semaphore("s_wkv2"))
        block = C(nc.Block())

        r = Rec(nc)
        for s in (stn, svc, ssc, sgp, s_const, s_dcs, s_xt0, s_xt1, s_wkv, s_qst,
                  s_qsl, s_att, s_atile, s_xot0, s_xot1, s_wtmp, s_wout, s_out0, s_out1, s_bias,
                  s_xt2, s_xt3, s_at1, s_qsl1, s_kx, s_kx2, s_xt4, s_xt5, s_xt6, s_xt7, s_qst1, s_att1, s_wkv2):
            r.sem(s)
        s_xot = [s_xot0, s_xot1]
        s_out = [s_out0, s_out1]

        es0 = ExitStack()
        bb = es0.enter_context(nc.sbuf_tensor("bb", [1, D], F32))
        onesf = es0.enter_context(nc.sbuf_tensor("onesf", [1, 128], F32))
        btmp = es0.enter_context(nc.sbuf_tensor("btmp", [1, D], F32))
        btmp2 = es0.enter_context(nc.sbuf_tensor("btmp2", [1, D], BF16))
        for dst, srcd in ((idb[:], idn), (bkqs[:], bkq), (scls[:], scl)):
            r.opi("sync", s_const, 16, "dma_start", dst, srcd)
        r.wait("vector", s_const)
        nc.const_aps.register(F32, EPS, epsc[:, 0:1])
        r.op("vector", "memset", epsc[:], EPS)
        r.op("vector", "memset", onesf[:], 1.0)
        r.op("vector", "memset", onesb2[:], 1.0)
        vb_ones = vbuf[:].rearrange("p (a c) -> p a c", c=65)[:, :, 64:65]
        r.op("vector", "memset", vb_ones, 1.0)
        r.op("vector", "reciprocal", sc_tmp[:, 0:1], scls[:, 0:1])
        r.op("vector", "tensor_tensor", sc_tmp[:, 1:2], scls[:, 0:1], scls[:, 1:2], AluOpType.mult)
        r.drain("vector")
        r.op("vector", "reciprocal", sc_tmp[:, 2:3], sc_tmp[:, 1:2])
        r.opi("vector", svc, 1, "tensor_scalar", sc_tmp[:, 3:4], sc_tmp[:, 1:2], sc_tmp[0:1, 1:2], None, AluOpType.mult)
        c_scal = r.val(svc)
        r.drain("vector")
        for b2, srcd, scale in ((bv2, bv, None), (bout2, bout, None), (bffn2, bffn, True)):
            r.wait("sync", svc, r.val(svc))
            r.opi("sync", s_const, 16, "dma_start", bb[:], srcd)
            r.wait("vector", s_const)
            if scale:
                r.op("vector", "tensor_scalar", bb[:], bb[:], sc_tmp[0:1, 2:3], None, AluOpType.mult)
                r.drain("vector")
            r.op("vector", "tensor_copy", b2[0:1, :], bb[:])
            r.drain("vector")
            r.op("vector", "tensor_tensor", btmp[:], bb[:], b2[0:1, :], AluOpType.subtract)
            r.drain("vector")
            r.wait("vector", s_bias, r.val(s_bias))
            r.opi("vector", svc, 1, "tensor_copy", btmp2[:], btmp[:])
            r.wait("gpsimd", svc, r.val(svc))
            r.opi("gpsimd", s_bias, 16, "dma_start", b2[1:2, :], btmp2[:])
        c_bias2 = r.val(svc)
        c_bias2d = r.val(s_bias)
        r.wait("tensor", svc, c_scal)
        r.opi("tensor", stn, 1, "matmul", psC[:, 0:4], onesf[:], sc_tmp[:], start=True, stop=True)
        c_bcast_tn = r.val(stn)
        r.wait("vector", stn, r.val(stn))
        r.opi("vector", svc, 1, "tensor_copy", scb[:], psC[:, 0:4])
        c_bcast = r.val(svc)
        r.flush(block)
        es0.close()
        rin_b = scb[:, 0:1]
        ts_b = scb[:, 1:2]
        ts2_b = scb[:, 3:4]

        esA = ExitStack()
        CA = esA.enter_context
        with esA:
            wkvb = CA(nc.sbuf_tensor("wkvb", [128, 8 * 2048], BF16))
            xts = [CA(nc.sbuf_tensor(f"xt{i}", [128, D], BF16)) for i in range(8)]
            xstg = [CA(nc.sbuf_tensor("xstg0", [128, 256], F32))]
            xnbs = [CA(nc.sbuf_tensor(f"xnb{i}", [128, D], BF16)) for i in range(2)]
            xnTs = [CA(nc.sbuf_tensor(f"xnT{i}", [128, 8 * 512], BF16)) for i in range(2)]
            lnst4s = [CA(nc.sbuf_tensor(f"lnst4{i}", [128, 48], F32)) for i in range(2)]
            lnmv4s = [CA(nc.sbuf_tensor(f"lnmv4{i}", [128, 8], F32)) for i in range(2)]
            lnsd4s = [CA(nc.sbuf_tensor(f"lnsd4{i}", [128, 4], F32)) for i in range(2)]
            qvs = [CA(nc.sbuf_tensor(f"qv{i}", [128, 512], BF16)) for i in range(2)]

            r.wait("gpsimd", s_bias, c_bias2d)
            r.wait("gpsimd", svc, c_bias2)
            r.wait("gpsimd", stn, c_bcast_tn)
            for dt in range(8):
                r.opi("gpsimd", s_wkv, 16, "dma_start",
                      wkvb[:, dt * 2048: dt * 2048 + 1024], wqkv[dt * 128:(dt + 1) * 128, D:2 * D])
            c_wkvK = r.val(s_wkv)
            for dt in range(8):
                r.opi("gpsimd", s_wkv2, 16, "dma_start",
                      wkvb[:, dt * 2048 + 1024:(dt + 1) * 2048], wqkv[dt * 128:(dt + 1) * 128, 2 * D:3 * D])
            c_wkvV = r.val(s_wkv2)

            s_xts = [s_xt0, s_xt1, s_xt2, s_xt3, s_xt4, s_xt5, s_xt6, s_xt7]
            xt_free = [0] * 8
            xstg_free = [0]
            xnb_used = [0] * 2
            xnT_mark = [0, 0]
            psC_evac = [0, 0]

            nrm_mark = {}

            def ln_vec(tc, srcs):
                cp = tc % 2
                bufs = [xts[cp * 4 + i] for i in range(4)]
                lnst4, lnmv4, lnsd4 = lnst4s[cp], lnmv4s[cp], lnsd4s[cp]
                for i in range(4):
                    b = cp * 4 + i
                    for hh in range(4):
                        r.wait("sync", svc, xstg_free[0])
                        r.opi("sync", s_xts[b], 16, "dma_start", xstg[0][:],
                              srcs[i][:, hh * 256:(hh + 1) * 256])
                        r.wait("vector", s_xts[b], r.val(s_xts[b]))
                        r.opi("vector", svc, 1, "tensor_copy",
                              bufs[i][:, hh * 256:(hh + 1) * 256], xstg[0][:])
                        xstg_free[0] = r.val(svc)
                    r.drain("vector")
                    r.op("vector", "bn_stats", lnst4[:, i * 12:i * 12 + 6], bufs[i][:, 0:512])
                    r.op("vector", "bn_stats", lnst4[:, i * 12 + 6:i * 12 + 12], bufs[i][:, 512:1024])
                r.drain("vector")
                for i in range(4):
                    r.op("vector", "bn_aggr", lnmv4[:, 2 * i:2 * i + 2], lnst4[:, i * 12:(i + 1) * 12])
                r.tag("vector", svc, 1)
                r.wait("scalar", svc, r.val(svc))
                var_v = lnmv4[:].rearrange("p (i two) -> p i two", two=2)[:, :, 1:2]
                sd_v = lnsd4[:].rearrange("p (i one) -> p i one", one=1)
                r.op("scalar", "add", sd_v, var_v, EPS)
                r.drain("scalar")
                r.opi("scalar", ssc, 1, "activation", lnsd4[:], lnsd4[:], AF.Sqrt)
                r.wait("vector", ssc, r.val(ssc))
                r.op("vector", "reciprocal", lnsd4[:], lnsd4[:])
                r.drain("vector")
                nrm_mark[tc] = r.val(svc)

            def tps_chunk(tc):
                cp = tc % 2
                bufs = [xts[cp * 4 + i] for i in range(4)]
                lnmv4, lnsd4 = lnmv4s[cp], lnsd4s[cp]
                for i in range(4):
                    nb = i % 2
                    b = cp * 4 + i
                    half_off = (i % 2) * 1024
                    r.wait("vector", stn, xnb_used[nb])
                    r.opi("vector", svc, 1, "tensor_scalar", xnbs[nb][:], bufs[i][:],
                          lnmv4[:, 2 * i:2 * i + 1], lnsd4[:, i:i + 1],
                          AluOpType.subtract, AluOpType.mult)
                    xt_free[b] = r.val(svc)
                    r.wait("tensor", svc, r.val(svc))
                    r.wait("tensor", ssc, psC_evac[i % 2])
                    for j in range(8):
                        r.op("tensor", "transpose", psCb[:, half_off + j * 128: half_off + (j + 1) * 128],
                             xnbs[nb][:, j * 128:(j + 1) * 128], idb[:])
                    r.tag("tensor", stn, 1)
                    xnb_used[nb] = r.val(stn)
                    r.wait("scalar", stn, xnb_used[nb])
                    for j in range(8):
                        r.op("scalar", "copy",
                             xnTs[cp][:, j * 512 + i * 128: j * 512 + i * 128 + 128],
                             psCb[:, half_off + j * 128: half_off + (j + 1) * 128])
                    r.tag("scalar", ssc, 1)
                    psC_evac[i % 2] = r.val(ssc)
                xnT_mark[cp] = r.val(ssc)

            r.barrier([svc, ssc, stn, s_const, s_bias])
            r.wait("tensor", s_const)
            r.wait("tensor", svc, c_bias2)
            r.wait("tensor", s_bias, c_bias2d)
            kfree = [r.val(ssc), r.val(ssc)]
            vfree = [r.val(ssc), r.val(ssc)]
            qv_free = [0, 0]
            s_qsts = [s_qst, s_qst1]

            def do_kv_proj(tc):
                xnT = xnTs[tc % 2]
                r.wait("tensor", s_wkv, c_wkvK)
                r.wait("tensor", ssc, xnT_mark[tc % 2])
                for ot in range(8):
                    reg = ot % 2
                    r.wait("tensor", ssc, kfree[reg])
                    for dt in range(8):
                        r.op("tensor", "matmul", psA[:, reg * 512:(reg + 1) * 512],
                             wkvb[:, dt * 2048 + ot * 128: dt * 2048 + (ot + 1) * 128],
                             xnT[:, dt * 512:(dt + 1) * 512],
                             start=(dt == 0), stop=(dt == 7))
                    r.tag("tensor", stn, 1)
                    r.wait("scalar", stn, r.val(stn))
                    r.opi("scalar", ssc, 1, "activation",
                          ktb[:, ot * L + tc * 512: ot * L + (tc + 1) * 512],
                          psA[:, reg * 512:(reg + 1) * 512],
                          AF.Identity, bias=bkqs[:, 8 + ot:9 + ot])
                    kfree[reg] = r.val(ssc)
                r.wait("tensor", s_wkv2, c_wkvV)
                for st in range(4):
                    for oc in range(2):
                        reg = (st * 2 + oc) % 2
                        r.wait("tensor", ssc, vfree[reg])
                        for dt in range(8):
                            r.op("tensor", "matmul", psB[:, reg * 512:(reg + 1) * 512],
                                 xnT[:, dt * 512 + st * 128: dt * 512 + (st + 1) * 128],
                                 wkvb[:, dt * 2048 + 1024 + oc * 512: dt * 2048 + 1024 + (oc + 1) * 512],
                                 start=(dt == 0), stop=False)
                        r.op("tensor", "matmul", psB[:, reg * 512:(reg + 1) * 512],
                             onesb2[:], bv2[:, oc * 512:(oc + 1) * 512], start=False, stop=True)
                        r.tag("tensor", stn, 1)
                        r.wait("scalar", stn, r.val(stn))
                        tile = tc * 4 + st
                        dst = vbuf[:, tile * (H * 65) + oc * 520: tile * (H * 65) + oc * 520 + 520]
                        dst = dst.rearrange("p (h c) -> p h c", c=65)[:, :, 0:64]
                        srcp = psB[:, reg * 512:(reg + 1) * 512].rearrange("p (h c) -> p h c", c=64)
                        r.opi("scalar", ssc, 1, "copy", dst, srcp)
                        vfree[reg] = r.val(ssc)

            def do_q_proj(tc):
                xnT = xnTs[tc % 2]
                r.wait("tensor", s_wkv, c_wq)
                r.wait("tensor", ssc, xnT_mark[tc % 2])
                for ot in range(8):
                    reg = ot % 2
                    qb = ot % 2
                    r.wait("tensor", ssc, kfree[reg])
                    for dt in range(8):
                        r.op("tensor", "matmul", psA[:, reg * 512:(reg + 1) * 512],
                             wkvb[:, dt * 1024 + ot * 128: dt * 1024 + (ot + 1) * 128],
                             xnT[:, dt * 512:(dt + 1) * 512],
                             start=(dt == 0), stop=(dt == 7))
                    r.tag("tensor", stn, 1)
                    r.wait("scalar", stn, r.val(stn))
                    r.wait("scalar", s_qsts[qb], qv_free[qb])
                    r.opi("scalar", ssc, 1, "activation", qvs[qb][:],
                          psA[:, reg * 512:(reg + 1) * 512],
                          AF.Identity, bias=bkqs[:, ot:ot + 1])
                    kfree[reg] = r.val(ssc)
                    r.wait("gpsimd", ssc, r.val(ssc))
                    r.opi("gpsimd", s_qsts[qb], 16, "dma_start",
                          qt_hbm[:, ot * HALF + tc * 512: ot * HALF + (tc + 1) * 512], qvs[qb][:])
                    qv_free[qb] = r.val(s_qsts[qb])

            def a_src(tc, st):
                return xf[(tc * 4 + st) * 128:(tc * 4 + st + 1) * 128, :]

            def o_src(tc, st):
                return xo[(tc * 4 + st) * 128:(tc * 4 + st + 1) * 128, :]

            ln_vec(0, [a_src(0, st) for st in range(4)])
            tps_chunk(0)
            for tc in range(8):
                if tc + 1 < 8:
                    ln_vec(tc + 1, [a_src(tc + 1, st) for st in range(4)])
                do_kv_proj(tc)
                if tc + 1 < 8:
                    tps_chunk(tc + 1)
            r.wait("gpsimd", stn, r.val(stn))
            for dt in range(8):
                r.opi("gpsimd", s_wkv, 16, "dma_start",
                      wkvb[:, dt * 1024:(dt + 1) * 1024], wqkv[dt * 128:(dt + 1) * 128, 0:D])
            c_wq = r.val(s_wkv)
            ln_vec(0, [o_src(0, st) for st in range(4)])
            tps_chunk(0)
            for tc in range(4):
                if tc + 1 < 4:
                    ln_vec(tc + 1, [o_src(tc + 1, st) for st in range(4)])
                do_q_proj(tc)
                if tc + 1 < 4:
                    tps_chunk(tc + 1)
            c_qdone = r.val(s_qst)
            c_qdone1 = r.val(s_qst1)
            c_a_tensor = r.val(stn)
            r.flush(block)

        esa = ExitStack()
        Ca = esa.enter_context
        with esa:
            dcs = Ca(nc.sbuf_tensor("dcs", [128, 1408], F32))
            u48s = Ca(nc.sbuf_tensor("u48s", [48, 4096], BF16))
            qrts = Ca(nc.sbuf_tensor("qrts", [96, 2048], BF16))
            qsl0 = Ca(nc.sbuf_tensor("qsl0", [128, 512], BF16))
            qsl1 = Ca(nc.sbuf_tensor("qsl1", [128, 512], BF16))
            kxE = Ca(nc.sbuf_tensor("kxE", [70, 3072], BF16))
            kxO = Ca(nc.sbuf_tensor("kxO", [70, 3072], BF16))
            qeEL = Ca(nc.sbuf_tensor("qeEL", [70, 512], BF16))
            qeEG = Ca(nc.sbuf_tensor("qeEG", [70, 512], BF16))
            qeOL = Ca(nc.sbuf_tensor("qeOL", [70, 512], BF16))
            qeOG = Ca(nc.sbuf_tensor("qeOG", [70, 512], BF16))
            ptA = Ca(nc.sbuf_tensor("ptA", [128, 1024], BF16))
            ptB = Ca(nc.sbuf_tensor("ptB", [128, 1024], BF16))
            ptC = Ca(nc.sbuf_tensor("ptC", [128, 1024], BF16))
            pvstack = Ca(nc.sbuf_tensor("pvstack", [128, 8 * 512 + 4096], BF16))
            rec16 = Ca(nc.sbuf_tensor("rec16", [128, 16], F32))
            rec16b = Ca(nc.sbuf_tensor("rec16b", [128, 16], F32))
            attq = Ca(nc.sbuf_tensor("attq", [128, 1024], BF16))
            attt = Ca(nc.sbuf_tensor("attt", [128, 1024], BF16))
            attt2 = Ca(nc.sbuf_tensor("attt2", [128, 1024], BF16))

            qsls = [qsl0, qsl1]
            s_qsls = [s_qsl, s_qsl1]
            SUM0 = 8 * 512

            r.barrier([stn, svc, ssc, s_xt0, s_xt1, s_xt2, s_xt3, s_wkv, s_qst, s_const])
            r.opi("sync", s_dcs, 16, "dma_start", dcs[:], dcore)
            r.opi("sync", s_dcs, 16, "dma_start", u48s[:], u48)
            r.opi("sync", s_dcs, 16, "dma_start", qrts[:], qrt)
            r.opi("sync", s_dcs, 16, "dma_start", kxE[67:70, :], ones_sgn[0:3, 0:3072])
            r.opi("sync", s_dcs, 16, "dma_start", kxO[67:70, :], ones_sgn[0:3, 0:3072])
            r.opi("sync", s_dcs, 16, "dma_start", qeEL[64:67, :], ones_sgn[0:3, 0:512])
            r.opi("sync", s_dcs, 16, "dma_start", qeOL[64:67, :], ones_sgn[0:3, 0:512])
            r.opi("sync", s_dcs, 16, "dma_start", qeEG[64:67, :], ones_sgn[3:6, 0:512])
            r.opi("sync", s_dcs, 16, "dma_start", qeOG[64:67, :], ones_sgn[3:6, 0:512])
            c_dcsl = r.val(s_dcs)
            r.wait("tensor", s_dcs, c_dcsl)
            r.wait("tensor", s_qst, c_qdone)
            r.wait("tensor", s_qst1, c_qdone1)
            r.wait("vector", s_dcs, c_dcsl)
            r.wait("sync", s_qst, c_qdone)
            r.wait("sync", s_qst1, c_qdone1)
            r.wait("sync", s_dcs, c_dcsl)
            r.wait("gpsimd", s_qst, c_qdone)
            r.wait("gpsimd", s_qst1, c_qdone1)
            r.wait("gpsimd", s_dcs, c_dcsl)
            r.wait("gpsimd", stn, r.val(stn))

            pss = [psA, psB, psC]
            pts = [ptA, ptB, ptC]
            qsl_cons = [0, 0]
            kxc = [0]
            for qc in range(NQC):
                stack_marks = []
                pair_meta = []
                stream = []
                for hp in range(8):
                    window, lts, ges = band_kts(hp, qc)
                    kts = window + lts + ges
                    pair_meta.append((hp, kts, len(window), len(lts), len(ges)))
                    for li, kt in enumerate(kts):
                        stream.append((hp, li, kt))
                G = len(stream)
                exp_done = {}
                stt_done = {}
                pv_done = {}
                mm_done = {}
                c_qload = {}
                c_kxp = {}

                def stage_qsl(p):
                    hp, kts, nw, nlt, nge = pair_meta[p]
                    qp = hp % 2
                    srcq = qt_hbm[:].rearrange("p (o t) -> p o t", o=8)[:, hp:hp + 1, qc * 512:(qc + 1) * 512]
                    r.wait("sync", stn, qsl_cons[qp])
                    r.opi("sync", s_qsls[qp], 16, "dma_start",
                          qsls[qp][:].rearrange("p (o t) -> p o t", o=1), srcq)
                    c_qload[p] = r.val(s_qsls[qp])

                def stage_kx(p):
                    hp, kts, nw, nlt, nge = pair_meta[p]
                    h0, h1 = 2 * hp, 2 * hp + 1
                    lts = kts[nw:nw + nlt]
                    ges = kts[nw + nlt:]
                    r.wait("gpsimd", stn, kxc[0])
                    r.wait("sync", stn, kxc[0])
                    for side_on, n_s, t0s in ((1, nlt, lts[0] if nlt else 0), (0, nge, ges[0] if nge else 0)):
                        if not n_s:
                            continue
                        off = 0 if side_on else 128 * nlt
                        r.opi("sync", s_kx2, 16, "dma_start", kxE[0:64, off:off + 128 * n_s],
                              ktb[0:64, hp * L + t0s * 128: hp * L + (t0s + n_s) * 128])
                        r.opi("gpsimd", s_kx, 16, "dma_start", kxO[0:64, off:off + 128 * n_s],
                              ktb[64:128, hp * L + t0s * 128: hp * L + (t0s + n_s) * 128])
                        r.opi("sync", s_kx2, 16, "dma_start", kxE[64:67, off:off + 128 * n_s],
                              u48s[3 * h0:3 * h0 + 3, t0s * 128:(t0s + n_s) * 128])
                        r.opi("gpsimd", s_kx, 16, "dma_start", kxO[64:67, off:off + 128 * n_s],
                              u48s[3 * h1:3 * h1 + 3, t0s * 128:(t0s + n_s) * 128])
                    for qe, rows0, qrow0, on_sync in ((qeEL, 0, 3 * h0, True), (qeOL, 64, 3 * h1, False),
                                                      (qeEG, 0, 48 + 3 * h0, True), (qeOG, 64, 48 + 3 * h1, False)):
                        eng = "sync" if on_sync else "gpsimd"
                        sem = s_kx2 if on_sync else s_kx
                        srcqq = qt_hbm[rows0:rows0 + 64, :].rearrange("p (o t) -> p o t", o=8)[:, hp:hp + 1, qc * 512:(qc + 1) * 512]
                        r.opi(eng, sem, 16, "dma_start",
                              qe[0:64, :].rearrange("p (o t) -> p o t", o=1), srcqq)
                        r.opi(eng, sem, 16, "dma_start", qe[67:70, :],
                              qrts[qrow0:qrow0 + 3, qc * 512:(qc + 1) * 512])
                    c_kxp[p] = (r.val(s_kx), r.val(s_kx2))

                def emit_mm(g):
                    p, li, kt = stream[g]
                    hp, kts, nw, nlt, nge = pair_meta[p]
                    qp = hp % 2
                    if li == 0:
                        if p == 0:
                            stage_qsl(0)
                        stage_kx(p)
                    X = pss[g % 3]
                    if li < nw:
                        if li == 0:
                            r.wait("tensor", s_qsls[qp], c_qload[p])
                        r.op("tensor", "matmul", X[:, 0:512],
                             ktb[0:64, hp * L + kt * 128: hp * L + (kt + 1) * 128],
                             qsls[qp][0:64, :],
                             start=True, stop=True, tile_position=(0, 0))
                        r.opi("tensor", stn, 1, "matmul", X[:, 512:1024],
                              ktb[64:128, hp * L + kt * 128: hp * L + (kt + 1) * 128],
                              qsls[qp][64:128, :],
                              start=True, stop=True, tile_position=(64, 0))
                        if li == nw - 1:
                            qsl_cons[qp] = r.val(stn)
                    else:
                        if li == nw:
                            r.wait("tensor", s_kx, c_kxp[p][0])
                            r.wait("tensor", s_kx2, c_kxp[p][1])
                        pos = li - nw
                        qeE = qeEL if pos < nlt else qeEG
                        qeO = qeOL if pos < nlt else qeOG
                        r.op("tensor", "matmul", X[:, 0:512],
                             kxE[:, pos * 128:(pos + 1) * 128], qeE[:],
                             start=True, stop=True)
                        r.opi("tensor", stn, 1, "matmul", X[:, 512:1024],
                              kxO[:, pos * 128:(pos + 1) * 128], qeO[:],
                              start=True, stop=True)
                    mm_done[g] = r.val(stn)
                    if li == len(kts) - 1:
                        kxc[0] = r.val(stn)

                for g in range(3):
                    emit_mm(g)
                for g in range(G):
                    p, li, kt = stream[g]
                    hp, kts, nw, nlt, nge = pair_meta[p]
                    h0, h1 = 2 * hp, 2 * hp + 1
                    s0 = float(SLOPES[h0])
                    s1 = float(SLOPES[h1])
                    n = len(kts)
                    if li == 0 and p + 1 < 8:
                        stage_qsl(p + 1)
                    X = pss[g % 3]
                    pt = pts[g % 3]
                    if li < nw:
                        jn = 896 - 128 * (kt - qc * 8)
                        r.wait("vector", stn, mm_done[g])
                        r.op("vector", "scalar_tensor_tensor",
                             X[:, 0:512], dcs[:, jn:jn + 512], s0, X[:, 0:512],
                             AluOpType.mult, AluOpType.add)
                        r.opi("vector", svc, 1, "scalar_tensor_tensor",
                              X[:, 512:1024], dcs[:, jn:jn + 512], s1, X[:, 512:1024],
                              AluOpType.mult, AluOpType.add)
                        stt_done[g] = r.val(svc)
                        r.wait("scalar", svc, stt_done[g])
                    else:
                        r.wait("scalar", stn, mm_done[g])
                    if g >= 3:
                        r.wait("scalar", stn, pv_done[g - 3])
                    r.opi("scalar", ssc, 1, "activation", pt[:], X[:], AF.Exp, scale=0.125)
                    exp_done[g] = r.val(ssc)
                    r.wait("tensor", ssc, exp_done[g])
                    vs0 = vbuf[:, kt * (H * 65) + h0 * 65: kt * (H * 65) + h0 * 65 + 65]
                    vs1 = vbuf[:, kt * (H * 65) + h1 * 65: kt * (H * 65) + h1 * 65 + 65]
                    r.op("tensor", "matmul", pvtA[0:65, :], vs0, pt[:, 0:512],
                         start=(li == 0), stop=(li == n - 1))
                    r.opi("tensor", stn, 1, "matmul", pvtB[0:65, :], vs1, pt[:, 512:1024],
                          start=(li == 0), stop=(li == n - 1))
                    pv_done[g] = r.val(stn)
                    if li == n - 1:
                        r.wait("scalar", stn, pv_done[g])
                        r.wait("vector", stn, pv_done[g])
                        r.op("scalar", "copy", pvstack[0:64, hp * 512:(hp + 1) * 512], pvtA[0:64, :])
                        r.opi("scalar", ssc, 1, "copy", pvstack[64:65, SUM0 + hp * 512: SUM0 + (hp + 1) * 512], pvtA[64:65, :])
                        r.op("vector", "tensor_copy", pvstack[64:128, hp * 512:(hp + 1) * 512], pvtB[0:64, :])
                        r.opi("vector", svc, 1, "tensor_copy", pvstack[0:1, SUM0 + hp * 512: SUM0 + (hp + 1) * 512], pvtB[64:65, :])
                        stack_marks.append((r.val(ssc), r.val(svc)))
                    if g + 3 < G:
                        emit_mm(g + 3)
                c_stack = r.val(ssc)
                c_stackv = r.val(svc)

                attts = [attt, attt2]
                recs = [rec16, rec16b]
                s_atts = [s_att, s_att1]
                nrm_done = {}
                for qt in range(4):
                    tb = pvtAb if qt % 2 == 0 else pvtBb
                    sb = pvtBb[:, 0:32] if qt % 2 == 0 else pvtAb[:, 0:32]
                    if qt == 0:
                        r.wait("tensor", ssc, c_stack)
                        r.wait("tensor", svc, c_stackv)
                        r.wait("tensor", svc, r.val(svc))
                        r.wait("tensor", ssc, r.val(ssc))
                    elif qt >= 2:
                        r.wait("tensor", svc, nrm_done[qt - 2])
                    for j in range(8):
                        r.op("tensor", "transpose", tb[:, j * 128:(j + 1) * 128],
                             pvstack[:, j * 512 + qt * 128: j * 512 + qt * 128 + 128], idb[:])
                    if qt >= 1:
                        r.wait("tensor", svc, nrm_done[qt - 1])
                    for h in range(H):
                        rowp = 64 if h % 2 == 0 else 0
                        srcs = pvstack[rowp:rowp + 1,
                                       SUM0 + (h // 2) * 512 + qt * 128: SUM0 + (h // 2) * 512 + qt * 128 + 128]
                        r.op("tensor", "transpose", sb[:, 2 * h:2 * h + 1], srcs,
                             idb[rowp:rowp + 1, rowp:rowp + 1])
                    r.tag("tensor", stn, 1)
                    c_tp = r.val(stn)
                    r.wait("vector", stn, c_tp)
                    sums_v = sb[:, 0:32].rearrange("p (h two) -> p h two", two=2)[:, :, 0:1]
                    rc = recs[qt % 2]
                    r.op("vector", "reciprocal", rc[:].rearrange("p (h one) -> p h one", one=1), sums_v)
                    r.drain("vector")
                    for h in range(H):
                        r.op("vector", "tensor_scalar",
                             attq[:, h * 64:(h + 1) * 64],
                             tb[:, (h // 2) * 128 + (h % 2) * 64: (h // 2) * 128 + (h % 2) * 64 + 64],
                             rc[:, h:h + 1], None, AluOpType.mult)
                    r.tag("vector", svc, 1)
                    c_nrm = r.val(svc)
                    r.wait("tensor", svc, c_nrm)
                    for j in range(8):
                        r.op("tensor", "transpose", tb[:, j * 128:(j + 1) * 128],
                             attq[:, j * 128:(j + 1) * 128], idb[:])
                    r.tag("tensor", stn, 1)
                    c_tp2 = r.val(stn)
                    r.wait("vector", s_atts[qt % 2], r.val(s_atts[qt % 2]))
                    r.wait("vector", stn, c_tp2)
                    for j in range(8):
                        r.op("vector", "tensor_copy", attts[qt % 2][:, j * 128:(j + 1) * 128],
                             tb[:, j * 128:(j + 1) * 128])
                    r.tag("vector", svc, 1)
                    nrm_done[qt] = r.val(svc)
                    r.wait("gpsimd", svc, r.val(svc))
                    tti = qc * 4 + qt
                    r.opi("gpsimd", s_atts[qt % 2], 16, "dma_start",
                          at_hbm[:, tti * 1024:(tti + 1) * 1024], attts[qt % 2][:])
            c_att_done = r.val(s_att)
            c_att_done1 = r.val(s_att1)
            c_att_tn = r.val(stn)
            r.flush(block)

        est = ExitStack()
        Ct = est.enter_context
        with est:
            woutb = Ct(nc.sbuf_tensor("woutb", [128, 8 * 1024], BF16))
            w8t = Ct(nc.sbuf_tensor("w8t", [128, 8 * 1024], U8))
            wtmp = Ct(nc.sbuf_tensor("wtmp", [128, 1024], F32))
            atile0 = Ct(nc.sbuf_tensor("atile0", [128, 1024], BF16))
            atile1 = Ct(nc.sbuf_tensor("atile1", [128, 1024], BF16))
            xot0 = Ct(nc.sbuf_tensor("xot0", [128, D], F32))
            xot1 = Ct(nc.sbuf_tensor("xot1", [128, D], F32))
            x1a = Ct(nc.sbuf_tensor("x1a", [128, D], F32))
            x1b = Ct(nc.sbuf_tensor("x1b", [128, D], F32))
            t1 = Ct(nc.sbuf_tensor("t1", [128, D], F32))
            xqb0 = Ct(nc.sbuf_tensor("xqb0", [128, D], BF16))
            xqb1 = Ct(nc.sbuf_tensor("xqb1", [128, D], BF16))
            x8c0 = Ct(nc.sbuf_tensor("x8c0", [128, D], U8))
            x8c1 = Ct(nc.sbuf_tensor("x8c1", [128, D], U8))
            outt0 = Ct(nc.sbuf_tensor("outt0", [128, D], F32))
            outt1 = Ct(nc.sbuf_tensor("outt1", [128, D], F32))
            lnst2 = Ct(nc.sbuf_tensor("lnst2", [128, 12], F32))
            lnmv2 = Ct(nc.sbuf_tensor("lnmv2", [128, 2], F32))
            lnsd2 = Ct(nc.sbuf_tensor("lnsd2", [128, 1], F32))
            lnstB = Ct(nc.sbuf_tensor("lnstB", [128, 12], F32))
            lnmvB = Ct(nc.sbuf_tensor("lnmvB", [128, 2], F32))
            lnsdB = Ct(nc.sbuf_tensor("lnsdB", [128, 1], F32))
            lnstC = Ct(nc.sbuf_tensor("lnstC", [128, 12], F32))
            lnmvC = Ct(nc.sbuf_tensor("lnmvC", [128, 2], F32))
            lnsdC = Ct(nc.sbuf_tensor("lnsdC", [128, 1], F32))
            t5 = Ct(nc.sbuf_tensor("t5", [128, D], F32))
            nb1 = Ct(nc.sbuf_tensor("nb1", [128, 1], F32))
            nb2 = Ct(nc.sbuf_tensor("nb2", [128, 1], F32))
            nb3 = Ct(nc.sbuf_tensor("nb3", [128, 1], F32))

            atiles = [atile0, atile1]
            s_atl = [s_atile, s_at1]
            xots = [xot0, xot1]
            x1s = [x1a, x1b]
            xqbs = [xqb0, xqb1]
            x8cs = [x8c0, x8c1]
            outts = [outt0, outt1]

            r.barrier([stn, svc, ssc, s_qsl, s_att, s_att1, s_dcs])
            for dt in range(8):
                r.opi("gpsimd", s_wout, 16, "dma_start",
                      woutb[:, dt * 1024:(dt + 1) * 1024], wout[dt * 128:(dt + 1) * 128, :])
            c_woutd = r.val(s_wout)
            r.wait("vector", svc, c_bcast)
            for it in range(8):
                r.opi("sync", s_wtmp, 16, "dma_start", wtmp[:], wfT[it * 128:(it + 1) * 128, :])
                r.wait("vector", s_wtmp, r.val(s_wtmp))
                r.opi("vector", svc, 1, "tensor_copy",
                      w8t[:, it * 1024:(it + 1) * 1024].bitcast(FP8), wtmp[:])
                r.wait("sync", svc, r.val(svc))
            c_w8 = r.val(svc)

            def ln_prep(src_ap):
                r.op("vector", "bn_stats", lnst2[:, 0:6], src_ap[:, 0:512])
                r.op("vector", "bn_stats", lnst2[:, 6:12], src_ap[:, 512:1024])
                r.drain("vector")
                r.op("vector", "bn_aggr", lnmv2[:], lnst2[:])
                r.tag("vector", svc, 1)
                r.wait("scalar", svc, r.val(svc))
                r.op("scalar", "add", lnsd2[:], lnmv2[:, 1:2], EPS)
                r.drain("scalar")
                r.opi("scalar", ssc, 1, "activation", lnsd2[:], lnsd2[:], AF.Sqrt)
                r.wait("vector", ssc, r.val(ssc))
                r.op("vector", "reciprocal", lnsd2[:], lnsd2[:])
                r.drain("vector")

            r.wait("sync", s_att, c_att_done)
            r.wait("sync", s_att1, c_att_done1)
            psW = [psA, psB]
            psWb = [psA[:].bitcast(BF16), psB[:].bitcast(BF16)]
            atile_cons = [0, 0]
            xot_free = [0, 0]
            mmW = [0, 0]
            t1act = [0, 0]
            aggrW = [0, 0]
            x1add = [0, 0]
            xqbact = [0, 0]
            transp = [0, 0]
            castm = [0, 0]
            mmF = [0, 0]
            aggrF = [0, 0]
            t1bact = [0, 0]
            c_x1add = [0]
            c_outadd = [0]
            load_marks = {}

            def psF_ap(pb, oc):
                if pb == 0:
                    return psC[:, oc * 512:(oc + 1) * 512]
                return pvtA[:] if oc == 0 else pvtB[:]

            def tail_load(tt):
                pb = tt % 2
                r.wait("sync", stn, atile_cons[pb])
                r.opi("sync", s_atl[pb], 16, "dma_start",
                      atiles[pb][:], at_hbm[:, tt * 1024:(tt + 1) * 1024])
                r.wait("sync", svc, xot_free[pb])
                r.opi("sync", s_xot[pb], 16, "dma_start", xots[pb][:], xo[tt * 128:(tt + 1) * 128, :])
                load_marks[tt] = (r.val(s_atl[pb]), r.val(s_xot[pb]))

            def S1(tt):
                pb = tt % 2
                r.wait("tensor", s_atl[pb], load_marks[tt][0])
                if tt == 0:
                    r.wait("tensor", s_wout, c_woutd)
                r.wait("tensor", svc, castm[pb])
                r.wait("tensor", ssc, t1act[pb])
                for oc in range(2):
                    for dt in range(8):
                        r.op("tensor", "matmul", psW[pb][:, oc * 512:(oc + 1) * 512],
                             atiles[pb][:, dt * 128:(dt + 1) * 128],
                             woutb[:, dt * 1024 + oc * 512: dt * 1024 + (oc + 1) * 512],
                             start=(dt == 0), stop=False)
                    r.op("tensor", "matmul", psW[pb][:, oc * 512:(oc + 1) * 512],
                         onesb2[:], bout2[:, oc * 512:(oc + 1) * 512], start=False, stop=True)
                r.tag("tensor", stn, 1)
                atile_cons[pb] = r.val(stn)
                mmW[pb] = r.val(stn)

            def S2(tt):
                pb = tt % 2
                r.wait("vector", stn, mmW[pb])
                r.op("vector", "bn_stats", lnst2[:, 0:6], psW[pb][:, 0:512])
                r.op("vector", "bn_stats", lnst2[:, 6:12], psW[pb][:, 512:1024])
                r.drain("vector")
                r.op("vector", "bn_aggr", lnmv2[:], lnst2[:])
                r.tag("vector", svc, 1)
                aggrW[pb] = r.val(svc)
                r.wait("scalar", svc, aggrW[pb])
                r.op("scalar", "add", lnsd2[:], lnmv2[:, 1:2], EPS)
                r.drain("scalar")
                r.opi("scalar", ssc, 1, "activation", lnsd2[:], lnsd2[:], AF.Sqrt)
                r.wait("vector", ssc, r.val(ssc))
                r.op("vector", "reciprocal", lnsd2[:], lnsd2[:])
                r.drain("vector")
                r.op("vector", "tensor_scalar", nb1[:], lnmv2[:, 0:1],
                     lnsd2[:, 0:1], -1.0, AluOpType.mult, AluOpType.mult)
                r.tag("vector", svc, 1)
                r.wait("scalar", svc, r.val(svc))
                r.wait("scalar", svc, c_x1add[0])
                r.opi("scalar", ssc, 1, "activation", t1[:], psW[pb][:], AF.Identity,
                      bias=nb1[:, 0:1], scale=lnsd2[:, 0:1])
                t1act[pb] = r.val(ssc)
                r.wait("vector", ssc, t1act[pb])
                r.wait("vector", s_xot[pb], load_marks[tt][1])
                r.opi("vector", svc, 1, "tensor_tensor", x1s[pb][:], t1[:], xots[pb][:], AluOpType.add)
                xot_free[pb] = r.val(svc)
                x1add[pb] = r.val(svc)
                c_x1add[0] = r.val(svc)

            def S3(tt):
                pb = tt % 2
                r.drain("vector")
                r.op("vector", "bn_stats", lnstB[:, 0:6], x1s[pb][:, 0:512])
                r.op("vector", "bn_stats", lnstB[:, 6:12], x1s[pb][:, 512:1024])
                r.drain("vector")
                r.op("vector", "bn_aggr", lnmvB[:], lnstB[:])
                r.tag("vector", svc, 1)
                r.wait("scalar", svc, r.val(svc))
                r.op("scalar", "add", lnsdB[:], lnmvB[:, 1:2], EPS)
                r.drain("scalar")
                r.opi("scalar", ssc, 1, "activation", lnsdB[:], lnsdB[:], AF.Sqrt)
                r.wait("vector", ssc, r.val(ssc))
                r.op("vector", "reciprocal", lnsdB[:], lnsdB[:])
                r.drain("vector")
                r.op("vector", "tensor_tensor", lnsdB[:], lnsdB[:], rin_b, AluOpType.mult)
                r.drain("vector")
                r.op("vector", "tensor_scalar", nb2[:], lnmvB[:, 0:1],
                     lnsdB[:, 0:1], -1.0, AluOpType.mult, AluOpType.mult)
                r.tag("vector", svc, 1)
                r.wait("scalar", svc, r.val(svc))
                r.wait("scalar", stn, transp[pb])
                r.opi("scalar", ssc, 1, "activation", xqbs[pb][:], x1s[pb][:], AF.Identity,
                      bias=nb2[:, 0:1], scale=lnsdB[:, 0:1])
                xqbact[pb] = r.val(ssc)
                r.wait("tensor", ssc, xqbact[pb])
                r.wait("tensor", svc, aggrW[pb])
                for j in range(8):
                    r.op("tensor", "transpose", psWb[pb][:, j * 128:(j + 1) * 128],
                         xqbs[pb][:, j * 128:(j + 1) * 128], idb[:])
                r.tag("tensor", stn, 1)
                transp[pb] = r.val(stn)
                r.wait("vector", stn, transp[pb])
                r.wait("vector", stn, mmF[pb])
                r.opi("vector", svc, 1, "tensor_copy", x8cs[pb][:].bitcast(FP8),
                      psWb[pb][:, 0:1024])
                castm[pb] = r.val(svc)

            def S4(tt):
                pb = tt % 2
                if tt == 0:
                    r.wait("tensor", svc, c_w8)
                r.wait("tensor", svc, castm[pb])
                r.wait("tensor", svc, aggrF[pb])
                r.wait("tensor", ssc, t1bact[pb])
                for oc in range(2):
                    dst = psF_ap(pb, oc)
                    for it in range(8):
                        r.op("tensor", "matmul", dst,
                             x8cs[pb][:, it * 128:(it + 1) * 128].bitcast(FP8),
                             w8t[:, it * 1024 + oc * 512: it * 1024 + (oc + 1) * 512].bitcast(FP8),
                             start=(it == 0), stop=False)
                    r.op("tensor", "matmul", dst,
                         onesb2[:], bffn2[:, oc * 512:(oc + 1) * 512], start=False, stop=True)
                r.tag("tensor", stn, 1)
                mmF[pb] = r.val(stn)

            def S5(tt):
                pb = tt % 2
                r.wait("vector", stn, mmF[pb])
                r.op("vector", "bn_stats", lnstC[:, 0:6], psF_ap(pb, 0))
                r.op("vector", "bn_stats", lnstC[:, 6:12], psF_ap(pb, 1))
                r.drain("vector")
                r.op("vector", "bn_aggr", lnmvC[:], lnstC[:])
                r.drain("vector")
                r.op("vector", "tensor_scalar", lnsdC[:], lnmvC[:, 1:2],
                     ts2_b, EPS, AluOpType.mult, AluOpType.add)
                r.tag("vector", svc, 1)
                r.wait("scalar", svc, r.val(svc))
                r.opi("scalar", ssc, 1, "activation", lnsdC[:], lnsdC[:], AF.Sqrt)
                r.wait("vector", ssc, r.val(ssc))
                r.op("vector", "reciprocal", lnsdC[:], lnsdC[:])
                r.drain("vector")
                r.op("vector", "tensor_tensor", lnsdC[:], lnsdC[:], ts_b, AluOpType.mult)
                r.drain("vector")
                r.op("vector", "tensor_scalar", nb3[:], lnmvC[:, 0:1],
                     lnsdC[:, 0:1], -1.0, AluOpType.mult, AluOpType.mult)
                r.tag("vector", svc, 1)
                aggrF[pb] = r.val(svc)
                r.wait("scalar", svc, r.val(svc))
                r.wait("scalar", svc, c_outadd[0])
                r.opi("scalar", ssc, 1, "activation", t5[:, 0:512], psF_ap(pb, 0), AF.Identity,
                      bias=nb3[:, 0:1], scale=lnsdC[:, 0:1])
                r.opi("scalar", ssc, 1, "activation", t5[:, 512:1024], psF_ap(pb, 1), AF.Identity,
                      bias=nb3[:, 0:1], scale=lnsdC[:, 0:1])
                t1bact[pb] = r.val(ssc)
                r.wait("vector", ssc, t1bact[pb])
                r.wait("vector", s_out[pb], r.val(s_out[pb]))
                r.opi("vector", svc, 1, "tensor_tensor", outts[pb][:], t5[:], x1s[pb][:], AluOpType.add)
                c_outadd[0] = r.val(svc)
                r.wait("sync", svc, c_outadd[0])
                r.opi("sync", s_out[pb], 16, "dma_start", yout[tt * 128:(tt + 1) * 128, :], outts[pb][:])

            tail_load(0)
            tail_load(1)
            S1(0)
            S2(0)
            S1(1)
            for k in range(NT_OWN):
                if k + 2 < NT_OWN:
                    tail_load(k + 2)
                S3(k)
                if k + 1 < NT_OWN:
                    S2(k + 1)
                S4(k)
                if k + 2 < NT_OWN:
                    S1(k + 2)
                S5(k)
            r.wait("sync", s_out0, r.val(s_out0))
            r.wait("sync", s_out1, r.val(s_out1))
            r.flush(block)

    return nc


_NC_CACHE = None


def _get_nc():
    global _NC_CACHE
    if _NC_CACHE is None:
        _NC_CACHE = build_nc()
    return _NC_CACHE


def _host_inputs(L0_attn_x, L0_attn_Wqkv, L0_attn_bqkv, L0_attn_Wout, L0_attn_bout,
                 L0_ffn_weight_fp8, L0_ffn_bias, L0_ffn_input_scale, L0_ffn_weight_scale):
    X = np.ascontiguousarray(np.asarray(L0_attn_x, dtype=np.float32))
    Wqkv = np.ascontiguousarray(np.asarray(L0_attn_Wqkv, dtype=np.float32))
    bqkv = np.asarray(L0_attn_bqkv, dtype=np.float32)
    Wout = np.ascontiguousarray(np.asarray(L0_attn_Wout, dtype=np.float32))
    bout = np.ascontiguousarray(np.asarray(L0_attn_bout, dtype=np.float32).reshape(1, D))
    WfT = np.ascontiguousarray(np.asarray(L0_ffn_weight_fp8, dtype=np.float32).T)
    bffn = np.ascontiguousarray(np.asarray(L0_ffn_bias, dtype=np.float32).reshape(1, D))
    scl = np.array([[float(np.asarray(L0_ffn_input_scale).reshape(-1)[0]),
                     float(np.asarray(L0_ffn_weight_scale).reshape(-1)[0])]], dtype=np.float32)
    bkq = np.ascontiguousarray(bqkv[0:2 * D].reshape(16, 128).T)
    bv = np.ascontiguousarray(bqkv[2 * D:3 * D].reshape(1, D))
    idn = np.eye(128, dtype=ml_dtypes.bfloat16)

    dcores = []
    qrts = []
    for half in range(2):
        jj = np.arange(1408, dtype=np.int64)
        p = np.arange(128, dtype=np.int64)
        m = (jj - 896 + half * 512)[None, :] - p[:, None]
        dcores.append(np.ascontiguousarray((-8.0 * np.abs(m)).astype(np.float32)))
        t = np.arange(HALF, dtype=np.int64)
        qabs = (2 * (t // 512) + half) * 512 + (t % 512)
        qr = np.zeros((96, HALF), dtype=ml_dtypes.bfloat16)
        for h in range(H):
            v = -8.0 * float(SLOPES[h]) * (qabs.astype(np.float64) - 2048.0)
            for j, term in enumerate(_bf16_split3(v)):
                qr[3 * h + j] = term
            vg = -v
            for j, term in enumerate(_bf16_split3(vg)):
                qr[48 + 3 * h + j] = term
        qrts.append(np.ascontiguousarray(qr))
    k = np.arange(L, dtype=np.int64)
    u48v = np.zeros((48, L), dtype=ml_dtypes.bfloat16)
    for h in range(H):
        v = 8.0 * float(SLOPES[h]) * (k.astype(np.float64) - 2048.0)
        for j, term in enumerate(_bf16_split3(v)):
            u48v[3 * h + j] = term
    u48v = np.ascontiguousarray(u48v)
    ones_sgn = np.zeros((6, 3072), dtype=ml_dtypes.bfloat16)
    ones_sgn[0:3] = 1.0
    ones_sgn[3:6] = -1.0

    in_maps = []
    for c in range(8):
        b, half = c // 2, c % 2
        xo_c = np.ascontiguousarray(X[b].reshape(8, 512, D)[half::2].reshape(HALF, D))
        in_maps.append(dict(
            xf=X[b], xo=xo_c,
            wqkv=Wqkv, bkq=bkq, bv=bv, wout=Wout, bout=bout,
            wfT=WfT, bffn=bffn, scl=scl, dcore=dcores[half], idn=idn,
            u48=u48v, qrt=qrts[half], ones_sgn=ones_sgn))
    return in_maps


def kernel(**inputs):
    nc = _get_nc()
    in_maps = _host_inputs(**inputs)
    res = run_bass_kernel_spmd(nc, in_maps, core_ids=list(range(8)))
    out = np.empty((B, L, D), dtype=np.float32)
    for c in range(8):
        b, half = c // 2, c % 2
        out[b].reshape(8, 512, D)[half::2] = res.results[c]["yout"].reshape(4, 512, D)
    return out


# version 28
# speedup vs baseline: 1.2218x; 1.2218x over previous
import math
import numpy as np
import ml_dtypes
import concourse.bass as bass
import concourse.mybir as mybir
from concourse.bass_utils import run_bass_kernel_spmd
from concourse.alu_op_type import AluOpType
from contextlib import ExitStack

F32 = mybir.dt.float32
BF16 = mybir.dt.bfloat16
FP8 = mybir.dt.float8e4
U8 = mybir.dt.uint8
AF = mybir.ActivationFunctionType

B, L, D, H, DH = 4, 4096, 1024, 16, 64
HALF = L // 2
NT_FULL = L // 128
NT_OWN = HALF // 128
NQC = HALF // 512
EPS = 1e-5


def _slopes():
    start = 2 ** (-(2 ** -(math.log2(H) - 3)))
    return [np.float32(start ** i) for i in range(H)]


SLOPES = _slopes()

DIST_CUT = 45.0


def band_kts(hp, qc):
    W = qc * 8
    window = list(range(W, W + 8))
    s = min(float(SLOPES[2 * hp]), float(SLOPES[2 * hp + 1]))
    dmax = DIST_CUT / s if DIST_CUT is not None else 1e9
    lt, ge = [], []
    for kt in range(NT_FULL):
        if W <= kt < W + 8:
            continue
        k_lo, k_hi = kt * 128, kt * 128 + 127
        ok = False
        for half in range(2):
            q_lo = (2 * qc + half) * 512
            q_hi = q_lo + 511
            if max(q_lo - k_hi, k_lo - q_hi, 0) <= dmax:
                ok = True
        if ok:
            (lt if kt < W else ge).append(kt)
    return window, lt, ge


def _bf16_split3(v):
    out = []
    rem = v.copy()
    for _ in range(3):
        t = rem.astype(ml_dtypes.bfloat16)
        out.append(t)
        rem = rem - t.astype(np.float64)
    return out


class Rec:
    ENGS = ("sync", "gpsimd", "tensor", "vector", "scalar")

    def __init__(self, nc):
        self.nc = nc
        self.q = {e: [] for e in self.ENGS}
        self.cnt = {}

    def sem(self, handle):
        self.cnt[id(handle)] = 0
        return handle

    def val(self, sem):
        return self.cnt[id(sem)]

    def op(self, eng, meth, *a, **k):
        self.q[eng].append((meth, a, k, None, 0))

    def opi(self, eng, sem, n, meth, *a, **k):
        self.cnt[id(sem)] += n
        self.q[eng].append((meth, a, k, sem, n))

    def tag(self, eng, sem, n):
        meth, a, k, s0, n0 = self.q[eng][-1]
        assert s0 is None, "last op already tagged"
        self.cnt[id(sem)] += n
        self.q[eng][-1] = (meth, a, k, sem, n)

    def wait(self, eng, sem, v=None):
        if v is None:
            v = self.cnt[id(sem)]
        self.q[eng].append(("wait_ge", (sem, v), {}, None, 0))

    def barrier(self, sems):
        vals = [(s, self.cnt[id(s)]) for s in sems]
        for eng in self.ENGS:
            for s, v in vals:
                self.q[eng].append(("wait_ge", (s, v), {}, None, 0))

    def drain(self, eng):
        self.q[eng].append(("drain", (), {}, None, 0))

    def flush(self, block):
        for ename in self.ENGS:
            ops = self.q[ename]
            if not ops:
                continue
            dec = getattr(block, ename)

            def make(ops_l):
                def body(e):
                    for meth, a, k, sem, n in ops_l:
                        r = getattr(e, meth)(*a, **k)
                        if sem is not None:
                            r.then_inc(sem, n)
                return body
            dec(make(list(ops)))
            self.q[ename] = []


def build_nc():
    nc = bass.Bass("TRN2", target_bir_lowering=False)

    xf = nc.dram_tensor("xf", [L, D], F32, kind="ExternalInput").ap()
    xo = nc.dram_tensor("xo", [HALF, D], F32, kind="ExternalInput").ap()
    wqkv = nc.dram_tensor("wqkv", [D, 3 * D], F32, kind="ExternalInput").ap()
    bkq = nc.dram_tensor("bkq", [128, 16], F32, kind="ExternalInput").ap()
    bv = nc.dram_tensor("bv", [1, D], F32, kind="ExternalInput").ap()
    wout = nc.dram_tensor("wout", [D, D], F32, kind="ExternalInput").ap()
    bout = nc.dram_tensor("bout", [1, D], F32, kind="ExternalInput").ap()
    wfT = nc.dram_tensor("wfT", [D, D], F32, kind="ExternalInput").ap()
    bffn = nc.dram_tensor("bffn", [1, D], F32, kind="ExternalInput").ap()
    scl = nc.dram_tensor("scl", [1, 2], F32, kind="ExternalInput").ap()
    dcore = nc.dram_tensor("dcore", [128, 1408], F32, kind="ExternalInput").ap()
    u48 = nc.dram_tensor("u48", [48, 4096], BF16, kind="ExternalInput").ap()
    qrt = nc.dram_tensor("qrt", [96, 2048], BF16, kind="ExternalInput").ap()
    ones_sgn = nc.dram_tensor("ones_sgn", [6, 3072], BF16, kind="ExternalInput").ap()
    idn = nc.dram_tensor("idn", [128, 128], BF16, kind="ExternalInput").ap()
    yout = nc.dram_tensor("yout", [HALF, D], F32, kind="ExternalOutput").ap()
    qt_hbm = nc.dram_tensor("qt_hbm", [128, 8 * HALF], BF16, kind="Internal").ap()
    at_hbm = nc.dram_tensor("at_hbm", [128, 8 * HALF], BF16, kind="Internal").ap()

    es = ExitStack()
    C = es.enter_context
    with es:
        ktb = C(nc.sbuf_tensor("ktb", [128, 8 * L], BF16))
        vbuf = C(nc.sbuf_tensor("vbuf", [128, NT_FULL * (H * 65)], BF16))
        idb = C(nc.sbuf_tensor("idb", [128, 128], BF16))
        idf = C(nc.sbuf_tensor("idf", [128, 128], F32))
        bkqs = C(nc.sbuf_tensor("bkqs", [128, 16], F32))
        scls = C(nc.sbuf_tensor("scls", [1, 2], F32))
        onesf = C(nc.sbuf_tensor("onesf", [1, 128], F32))
        onesb2 = C(nc.sbuf_tensor("onesb2", [2, 128], BF16))
        bv2 = C(nc.sbuf_tensor("bv2", [2, D], BF16))
        bout2 = C(nc.sbuf_tensor("bout2", [2, D], BF16))
        bffn2 = C(nc.sbuf_tensor("bffn2", [2, D], BF16))

        scb = C(nc.sbuf_tensor("scb", [128, 4], F32))
        sc_tmp = C(nc.sbuf_tensor("sc_tmp", [1, 4], F32))
        epsc = C(nc.sbuf_tensor("epsc", [128, 1], F32))

        psA = C(nc.psum_tensor("psA", [128, 1024], F32))
        psB = C(nc.psum_tensor("psB", [128, 1024], F32))
        pvtA = C(nc.psum_tensor("pvtA", [128, 512], F32))
        pvtB = C(nc.psum_tensor("pvtB", [128, 512], F32))
        pvtAb = pvtA[:].bitcast(BF16)
        pvtBb = pvtB[:].bitcast(BF16)
        psC = C(nc.psum_tensor("psC", [128, 1024], F32))
        psCb = psC[:].bitcast(BF16)

        stn = C(nc.semaphore("stn"))
        svc = C(nc.semaphore("svc"))
        ssc = C(nc.semaphore("ssc"))
        sgp = C(nc.semaphore("sgp"))
        s_const = C(nc.semaphore("s_const"))
        s_dcs = C(nc.semaphore("s_dcs"))
        s_xt0 = C(nc.semaphore("s_xt0"))
        s_xt1 = C(nc.semaphore("s_xt1"))
        s_xt2 = C(nc.semaphore("s_xt2"))
        s_xt3 = C(nc.semaphore("s_xt3"))
        s_at1 = C(nc.semaphore("s_at1"))
        s_qsl1 = C(nc.semaphore("s_qsl1"))
        s_kx = C(nc.semaphore("s_kx"))
        s_kx2 = C(nc.semaphore("s_kx2"))
        s_xt4 = C(nc.semaphore("s_xt4"))
        s_xt5 = C(nc.semaphore("s_xt5"))
        s_xt6 = C(nc.semaphore("s_xt6"))
        s_xt7 = C(nc.semaphore("s_xt7"))
        s_qst1 = C(nc.semaphore("s_qst1"))
        s_wkv = C(nc.semaphore("s_wkv"))
        s_qst = C(nc.semaphore("s_qst"))
        s_qsl = C(nc.semaphore("s_qsl"))
        s_att = C(nc.semaphore("s_att"))
        s_atile = C(nc.semaphore("s_atile"))
        s_xot0 = C(nc.semaphore("s_xot0"))
        s_xot1 = C(nc.semaphore("s_xot1"))
        s_wtmp = C(nc.semaphore("s_wtmp"))
        s_wout = C(nc.semaphore("s_wout"))
        s_out0 = C(nc.semaphore("s_out0"))
        s_out1 = C(nc.semaphore("s_out1"))
        s_bias = C(nc.semaphore("s_bias"))
        s_att1 = C(nc.semaphore("s_att1"))
        block = C(nc.Block())

        r = Rec(nc)
        for s in (stn, svc, ssc, sgp, s_const, s_dcs, s_xt0, s_xt1, s_wkv, s_qst,
                  s_qsl, s_att, s_atile, s_xot0, s_xot1, s_wtmp, s_wout, s_out0, s_out1, s_bias,
                  s_xt2, s_xt3, s_at1, s_qsl1, s_kx, s_kx2, s_xt4, s_xt5, s_xt6, s_xt7, s_qst1, s_att1):
            r.sem(s)
        s_xot = [s_xot0, s_xot1]
        s_out = [s_out0, s_out1]

        es0 = ExitStack()
        bb = es0.enter_context(nc.sbuf_tensor("bb", [1, D], F32))
        btmp = es0.enter_context(nc.sbuf_tensor("btmp", [1, D], F32))
        btmp2 = es0.enter_context(nc.sbuf_tensor("btmp2", [1, D], BF16))
        for dst, srcd in ((idb[:], idn), (bkqs[:], bkq), (scls[:], scl)):
            r.opi("sync", s_const, 16, "dma_start", dst, srcd)
        r.wait("vector", s_const)
        nc.const_aps.register(F32, EPS, epsc[:, 0:1])
        r.op("vector", "memset", epsc[:], EPS)
        r.op("vector", "tensor_copy", idf[:], idb[:])
        r.op("vector", "memset", onesf[:], 1.0)
        r.op("vector", "memset", onesb2[:], 1.0)
        vb_ones = vbuf[:].rearrange("p (a c) -> p a c", c=65)[:, :, 64:65]
        r.op("vector", "memset", vb_ones, 1.0)
        r.op("vector", "reciprocal", sc_tmp[:, 0:1], scls[:, 0:1])
        r.op("vector", "tensor_tensor", sc_tmp[:, 1:2], scls[:, 0:1], scls[:, 1:2], AluOpType.mult)
        r.drain("vector")
        r.op("vector", "reciprocal", sc_tmp[:, 2:3], sc_tmp[:, 1:2])
        r.opi("vector", svc, 1, "tensor_scalar", sc_tmp[:, 3:4], sc_tmp[:, 1:2], sc_tmp[0:1, 1:2], None, AluOpType.mult)
        c_scal = r.val(svc)
        r.drain("vector")
        for b2, srcd, scale in ((bv2, bv, None), (bout2, bout, None), (bffn2, bffn, True)):
            r.wait("sync", svc, r.val(svc))
            r.opi("sync", s_const, 16, "dma_start", bb[:], srcd)
            r.wait("vector", s_const)
            if scale:
                r.op("vector", "tensor_scalar", bb[:], bb[:], sc_tmp[0:1, 2:3], None, AluOpType.mult)
                r.drain("vector")
            r.op("vector", "tensor_copy", b2[0:1, :], bb[:])
            r.drain("vector")
            r.op("vector", "tensor_tensor", btmp[:], bb[:], b2[0:1, :], AluOpType.subtract)
            r.drain("vector")
            r.wait("vector", s_bias, r.val(s_bias))
            r.opi("vector", svc, 1, "tensor_copy", btmp2[:], btmp[:])
            r.wait("gpsimd", svc, r.val(svc))
            r.opi("gpsimd", s_bias, 16, "dma_start", b2[1:2, :], btmp2[:])
        c_bias2 = r.val(svc)
        c_bias2d = r.val(s_bias)
        r.wait("tensor", svc, c_scal)
        r.opi("tensor", stn, 1, "matmul", psC[:, 0:4], onesf[:], sc_tmp[:], start=True, stop=True)
        r.wait("vector", stn, r.val(stn))
        r.opi("vector", svc, 1, "tensor_copy", scb[:], psC[:, 0:4])
        c_bcast = r.val(svc)
        r.flush(block)
        es0.close()
        rin_b = scb[:, 0:1]
        ts_b = scb[:, 1:2]
        ts2_b = scb[:, 3:4]

        esA = ExitStack()
        CA = esA.enter_context
        with esA:
            wkvb = CA(nc.sbuf_tensor("wkvb", [128, 8 * 2048], BF16))
            xts = [CA(nc.sbuf_tensor(f"xt{i}", [128, D], BF16)) for i in range(8)]
            xnbs = [CA(nc.sbuf_tensor(f"xnb{i}", [128, D], BF16)) for i in range(2)]
            xnTs = [CA(nc.sbuf_tensor(f"xnT{i}", [128, 8 * 512], BF16)) for i in range(2)]
            lnst4s = [CA(nc.sbuf_tensor(f"lnst4{i}", [128, 48], F32)) for i in range(2)]
            lnmv4s = [CA(nc.sbuf_tensor(f"lnmv4{i}", [128, 8], F32)) for i in range(2)]
            lnsd4s = [CA(nc.sbuf_tensor(f"lnsd4{i}", [128, 4], F32)) for i in range(2)]
            qvs = [CA(nc.sbuf_tensor(f"qv{i}", [128, 512], BF16)) for i in range(2)]

            r.wait("gpsimd", s_bias, c_bias2d)
            r.wait("gpsimd", svc, c_bias2)
            for dt in range(8):
                r.opi("gpsimd", s_wkv, 16, "dma_start",
                      wkvb[:, dt * 2048:(dt + 1) * 2048], wqkv[dt * 128:(dt + 1) * 128, D:3 * D])
            c_wkv = r.val(s_wkv)

            s_xts = [s_xt0, s_xt1, s_xt2, s_xt3, s_xt4, s_xt5, s_xt6, s_xt7]
            xt_free = [0] * 8
            xnb_used = [0] * 2
            xnT_mark = [0, 0]
            psC_evac = [0, 0]

            nrm_mark = {}

            def ln_vec(tc, srcs):
                cp = tc % 2
                bufs = [xts[cp * 4 + i] for i in range(4)]
                lnst4, lnmv4, lnsd4 = lnst4s[cp], lnmv4s[cp], lnsd4s[cp]
                for i in range(4):
                    b = cp * 4 + i
                    r.wait("gpsimd", svc, xt_free[b])
                    r.opi("gpsimd", s_xts[b], 16, "dma_start", bufs[i][:], srcs[i])
                for i in range(4):
                    b = cp * 4 + i
                    r.wait("vector", s_xts[b])
                    r.op("vector", "bn_stats", lnst4[:, i * 12:i * 12 + 6], bufs[i][:, 0:512])
                    r.op("vector", "bn_stats", lnst4[:, i * 12 + 6:i * 12 + 12], bufs[i][:, 512:1024])
                r.drain("vector")
                for i in range(4):
                    r.op("vector", "bn_aggr", lnmv4[:, 2 * i:2 * i + 2], lnst4[:, i * 12:(i + 1) * 12])
                r.tag("vector", svc, 1)
                r.wait("scalar", svc, r.val(svc))
                var_v = lnmv4[:].rearrange("p (i two) -> p i two", two=2)[:, :, 1:2]
                sd_v = lnsd4[:].rearrange("p (i one) -> p i one", one=1)
                r.op("scalar", "add", sd_v, var_v, EPS)
                r.drain("scalar")
                r.opi("scalar", ssc, 1, "activation", lnsd4[:], lnsd4[:], AF.Sqrt)
                r.wait("vector", ssc, r.val(ssc))
                r.op("vector", "reciprocal", lnsd4[:], lnsd4[:])
                r.drain("vector")
                nrm_mark[tc] = r.val(svc)

            def tps_chunk(tc):
                cp = tc % 2
                bufs = [xts[cp * 4 + i] for i in range(4)]
                lnmv4, lnsd4 = lnmv4s[cp], lnsd4s[cp]
                for i in range(4):
                    nb = i % 2
                    b = cp * 4 + i
                    half_off = (i % 2) * 1024
                    r.wait("vector", stn, xnb_used[nb])
                    r.opi("vector", svc, 1, "tensor_scalar", xnbs[nb][:], bufs[i][:],
                          lnmv4[:, 2 * i:2 * i + 1], lnsd4[:, i:i + 1],
                          AluOpType.subtract, AluOpType.mult)
                    xt_free[b] = r.val(svc)
                    r.wait("tensor", svc, r.val(svc))
                    r.wait("tensor", ssc, psC_evac[i % 2])
                    for j in range(8):
                        r.op("tensor", "transpose", psCb[:, half_off + j * 128: half_off + (j + 1) * 128],
                             xnbs[nb][:, j * 128:(j + 1) * 128], idb[:])
                    r.tag("tensor", stn, 1)
                    xnb_used[nb] = r.val(stn)
                    r.wait("scalar", stn, xnb_used[nb])
                    for j in range(8):
                        r.op("scalar", "copy",
                             xnTs[cp][:, j * 512 + i * 128: j * 512 + i * 128 + 128],
                             psCb[:, half_off + j * 128: half_off + (j + 1) * 128])
                    r.tag("scalar", ssc, 1)
                    psC_evac[i % 2] = r.val(ssc)
                xnT_mark[cp] = r.val(ssc)

            r.barrier([svc, ssc, stn, s_const, s_bias])
            r.wait("tensor", s_const)
            r.wait("tensor", svc, c_bias2)
            r.wait("tensor", s_bias, c_bias2d)
            kfree = [r.val(ssc), r.val(ssc)]
            vfree = [r.val(ssc), r.val(ssc)]
            qv_free = [0, 0]
            s_qsts = [s_qst, s_qst1]

            def do_kv_proj(tc):
                xnT = xnTs[tc % 2]
                r.wait("tensor", s_wkv, c_wkv)
                r.wait("tensor", ssc, xnT_mark[tc % 2])
                for ot in range(8):
                    reg = ot % 2
                    r.wait("tensor", ssc, kfree[reg])
                    for dt in range(8):
                        r.op("tensor", "matmul", psA[:, reg * 512:(reg + 1) * 512],
                             wkvb[:, dt * 2048 + ot * 128: dt * 2048 + (ot + 1) * 128],
                             xnT[:, dt * 512:(dt + 1) * 512],
                             start=(dt == 0), stop=(dt == 7))
                    r.tag("tensor", stn, 1)
                    r.wait("scalar", stn, r.val(stn))
                    r.opi("scalar", ssc, 1, "activation",
                          ktb[:, ot * L + tc * 512: ot * L + (tc + 1) * 512],
                          psA[:, reg * 512:(reg + 1) * 512],
                          AF.Identity, bias=bkqs[:, 8 + ot:9 + ot])
                    kfree[reg] = r.val(ssc)
                for st in range(4):
                    for oc in range(2):
                        reg = (st * 2 + oc) % 2
                        r.wait("tensor", ssc, vfree[reg])
                        for dt in range(8):
                            r.op("tensor", "matmul", psB[:, reg * 512:(reg + 1) * 512],
                                 xnT[:, dt * 512 + st * 128: dt * 512 + (st + 1) * 128],
                                 wkvb[:, dt * 2048 + 1024 + oc * 512: dt * 2048 + 1024 + (oc + 1) * 512],
                                 start=(dt == 0), stop=False)
                        r.op("tensor", "matmul", psB[:, reg * 512:(reg + 1) * 512],
                             onesb2[:], bv2[:, oc * 512:(oc + 1) * 512], start=False, stop=True)
                        r.tag("tensor", stn, 1)
                        r.wait("scalar", stn, r.val(stn))
                        tile = tc * 4 + st
                        dst = vbuf[:, tile * (H * 65) + oc * 520: tile * (H * 65) + oc * 520 + 520]
                        dst = dst.rearrange("p (h c) -> p h c", c=65)[:, :, 0:64]
                        srcp = psB[:, reg * 512:(reg + 1) * 512].rearrange("p (h c) -> p h c", c=64)
                        r.opi("scalar", ssc, 1, "copy", dst, srcp)
                        vfree[reg] = r.val(ssc)

            def do_q_proj(tc):
                xnT = xnTs[tc % 2]
                r.wait("tensor", s_wkv, c_wq)
                r.wait("tensor", ssc, xnT_mark[tc % 2])
                for ot in range(8):
                    reg = ot % 2
                    qb = ot % 2
                    r.wait("tensor", ssc, kfree[reg])
                    for dt in range(8):
                        r.op("tensor", "matmul", psA[:, reg * 512:(reg + 1) * 512],
                             wkvb[:, dt * 1024 + ot * 128: dt * 1024 + (ot + 1) * 128],
                             xnT[:, dt * 512:(dt + 1) * 512],
                             start=(dt == 0), stop=(dt == 7))
                    r.tag("tensor", stn, 1)
                    r.wait("scalar", stn, r.val(stn))
                    r.wait("scalar", s_qsts[qb], qv_free[qb])
                    r.opi("scalar", ssc, 1, "activation", qvs[qb][:],
                          psA[:, reg * 512:(reg + 1) * 512],
                          AF.Identity, bias=bkqs[:, ot:ot + 1])
                    kfree[reg] = r.val(ssc)
                    r.wait("gpsimd", ssc, r.val(ssc))
                    r.opi("gpsimd", s_qsts[qb], 16, "dma_start",
                          qt_hbm[:, ot * HALF + tc * 512: ot * HALF + (tc + 1) * 512], qvs[qb][:])
                    qv_free[qb] = r.val(s_qsts[qb])

            def a_src(tc, st):
                return xf[(tc * 4 + st) * 128:(tc * 4 + st + 1) * 128, :]

            def o_src(tc, st):
                return xo[(tc * 4 + st) * 128:(tc * 4 + st + 1) * 128, :]

            ln_vec(0, [a_src(0, st) for st in range(4)])
            tps_chunk(0)
            for tc in range(8):
                if tc + 1 < 8:
                    ln_vec(tc + 1, [a_src(tc + 1, st) for st in range(4)])
                do_kv_proj(tc)
                if tc + 1 < 8:
                    tps_chunk(tc + 1)
            r.wait("gpsimd", stn, r.val(stn))
            for dt in range(8):
                r.opi("gpsimd", s_wkv, 16, "dma_start",
                      wkvb[:, dt * 1024:(dt + 1) * 1024], wqkv[dt * 128:(dt + 1) * 128, 0:D])
            c_wq = r.val(s_wkv)
            ln_vec(0, [o_src(0, st) for st in range(4)])
            tps_chunk(0)
            for tc in range(4):
                if tc + 1 < 4:
                    ln_vec(tc + 1, [o_src(tc + 1, st) for st in range(4)])
                do_q_proj(tc)
                if tc + 1 < 4:
                    tps_chunk(tc + 1)
            c_qdone = r.val(s_qst)
            c_qdone1 = r.val(s_qst1)
            c_a_tensor = r.val(stn)
            r.flush(block)

        esa = ExitStack()
        Ca = esa.enter_context
        with esa:
            woutb = Ca(nc.sbuf_tensor("woutb", [128, 8 * 1024], BF16))
            esatt = ExitStack()
            Ca = esatt.enter_context
            dcs = Ca(nc.sbuf_tensor("dcs", [128, 1408], F32))
            qsl0 = Ca(nc.sbuf_tensor("qsl0", [128, 512], BF16))
            qsl1 = Ca(nc.sbuf_tensor("qsl1", [128, 512], BF16))
            kxE = Ca(nc.sbuf_tensor("kxE", [70, 3072], BF16))
            kxO = Ca(nc.sbuf_tensor("kxO", [70, 3072], BF16))
            qeEL = Ca(nc.sbuf_tensor("qeEL", [70, 512], BF16))
            qeEG = Ca(nc.sbuf_tensor("qeEG", [70, 512], BF16))
            qeOL = Ca(nc.sbuf_tensor("qeOL", [70, 512], BF16))
            qeOG = Ca(nc.sbuf_tensor("qeOG", [70, 512], BF16))
            ptA = Ca(nc.sbuf_tensor("ptA", [128, 1024], BF16))
            ptB = Ca(nc.sbuf_tensor("ptB", [128, 1024], BF16))
            ptC = Ca(nc.sbuf_tensor("ptC", [128, 1024], BF16))
            pvstack = Ca(nc.sbuf_tensor("pvstack", [128, 8 * 512 + 4096], BF16))
            rec16 = Ca(nc.sbuf_tensor("rec16", [128, 16], F32))
            rec16b = Ca(nc.sbuf_tensor("rec16b", [128, 16], F32))
            attq = Ca(nc.sbuf_tensor("attq", [128, 1024], BF16))
            attt = Ca(nc.sbuf_tensor("attt", [128, 1024], BF16))
            attt2 = Ca(nc.sbuf_tensor("attt2", [128, 1024], BF16))

            qsls = [qsl0, qsl1]
            s_qsls = [s_qsl, s_qsl1]
            SUM0 = 8 * 512

            r.barrier([stn, svc, ssc, s_xt0, s_xt1, s_xt2, s_xt3, s_wkv, s_qst, s_const])
            r.opi("sync", s_dcs, 16, "dma_start", dcs[:], dcore)
            r.opi("sync", s_dcs, 16, "dma_start", kxE[67:70, :], ones_sgn[0:3, 0:3072])
            r.opi("sync", s_dcs, 16, "dma_start", kxO[67:70, :], ones_sgn[0:3, 0:3072])
            r.opi("sync", s_dcs, 16, "dma_start", qeEL[64:67, :], ones_sgn[0:3, 0:512])
            r.opi("sync", s_dcs, 16, "dma_start", qeOL[64:67, :], ones_sgn[0:3, 0:512])
            r.opi("sync", s_dcs, 16, "dma_start", qeEG[64:67, :], ones_sgn[3:6, 0:512])
            r.opi("sync", s_dcs, 16, "dma_start", qeOG[64:67, :], ones_sgn[3:6, 0:512])
            c_dcsl = r.val(s_dcs)
            r.wait("tensor", s_dcs, c_dcsl)
            r.wait("tensor", s_qst, c_qdone)
            r.wait("tensor", s_qst1, c_qdone1)
            r.wait("vector", s_dcs, c_dcsl)
            r.wait("sync", s_qst, c_qdone)
            r.wait("sync", s_qst1, c_qdone1)
            r.wait("sync", s_dcs, c_dcsl)
            r.wait("gpsimd", s_qst, c_qdone)
            r.wait("gpsimd", s_qst1, c_qdone1)
            r.wait("gpsimd", s_dcs, c_dcsl)
            r.wait("gpsimd", stn, r.val(stn))

            pss = [psA, psB, psC]
            pts = [ptA, ptB, ptC]
            qsl_cons = [0, 0]
            kxc = [0]
            for qc in range(NQC):
                stack_marks = []
                pair_meta = []
                stream = []
                for hp in range(8):
                    window, lts, ges = band_kts(hp, qc)
                    kts = window + lts + ges
                    pair_meta.append((hp, kts, len(window), len(lts), len(ges)))
                    for li, kt in enumerate(kts):
                        stream.append((hp, li, kt))
                G = len(stream)
                exp_done = {}
                stt_done = {}
                pv_done = {}
                mm_done = {}
                c_qload = {}
                c_kxp = {}

                def stage_qsl(p):
                    hp, kts, nw, nlt, nge = pair_meta[p]
                    qp = hp % 2
                    srcq = qt_hbm[:].rearrange("p (o t) -> p o t", o=8)[:, hp:hp + 1, qc * 512:(qc + 1) * 512]
                    r.wait("sync", stn, qsl_cons[qp])
                    r.opi("sync", s_qsls[qp], 16, "dma_start",
                          qsls[qp][:].rearrange("p (o t) -> p o t", o=1), srcq)
                    c_qload[p] = r.val(s_qsls[qp])

                def stage_kx(p):
                    hp, kts, nw, nlt, nge = pair_meta[p]
                    h0, h1 = 2 * hp, 2 * hp + 1
                    lts = kts[nw:nw + nlt]
                    ges = kts[nw + nlt:]
                    r.wait("gpsimd", stn, kxc[0])
                    r.wait("sync", stn, kxc[0])
                    for side_on, n_s, t0s in ((1, nlt, lts[0] if nlt else 0), (0, nge, ges[0] if nge else 0)):
                        if not n_s:
                            continue
                        off = 0 if side_on else 128 * nlt
                        r.opi("sync", s_kx2, 16, "dma_start", kxE[0:64, off:off + 128 * n_s],
                              ktb[0:64, hp * L + t0s * 128: hp * L + (t0s + n_s) * 128])
                        r.opi("gpsimd", s_kx, 16, "dma_start", kxO[0:64, off:off + 128 * n_s],
                              ktb[64:128, hp * L + t0s * 128: hp * L + (t0s + n_s) * 128])
                        r.opi("sync", s_kx2, 16, "dma_start", kxE[64:67, off:off + 128 * n_s],
                              u48[3 * h0:3 * h0 + 3, t0s * 128:(t0s + n_s) * 128])
                        r.opi("gpsimd", s_kx, 16, "dma_start", kxO[64:67, off:off + 128 * n_s],
                              u48[3 * h1:3 * h1 + 3, t0s * 128:(t0s + n_s) * 128])
                    for qe, rows0, qrow0, on_sync in ((qeEL, 0, 3 * h0, True), (qeOL, 64, 3 * h1, False),
                                                      (qeEG, 0, 48 + 3 * h0, True), (qeOG, 64, 48 + 3 * h1, False)):
                        eng = "sync" if on_sync else "gpsimd"
                        sem = s_kx2 if on_sync else s_kx
                        srcqq = qt_hbm[rows0:rows0 + 64, :].rearrange("p (o t) -> p o t", o=8)[:, hp:hp + 1, qc * 512:(qc + 1) * 512]
                        r.opi(eng, sem, 16, "dma_start",
                              qe[0:64, :].rearrange("p (o t) -> p o t", o=1), srcqq)
                        r.opi(eng, sem, 16, "dma_start", qe[67:70, :],
                              qrt[qrow0:qrow0 + 3, qc * 512:(qc + 1) * 512])
                    c_kxp[p] = (r.val(s_kx), r.val(s_kx2))

                def emit_mm(g):
                    p, li, kt = stream[g]
                    hp, kts, nw, nlt, nge = pair_meta[p]
                    qp = hp % 2
                    if li == 0:
                        if p == 0:
                            stage_qsl(0)
                        stage_kx(p)
                    X = pss[g % 3]
                    if li < nw:
                        if li == 0:
                            r.wait("tensor", s_qsls[qp], c_qload[p])
                        r.op("tensor", "matmul", X[:, 0:512],
                             ktb[0:64, hp * L + kt * 128: hp * L + (kt + 1) * 128],
                             qsls[qp][0:64, :],
                             start=True, stop=True, tile_position=(0, 0))
                        r.opi("tensor", stn, 1, "matmul", X[:, 512:1024],
                              ktb[64:128, hp * L + kt * 128: hp * L + (kt + 1) * 128],
                              qsls[qp][64:128, :],
                              start=True, stop=True, tile_position=(64, 0))
                        if li == nw - 1:
                            qsl_cons[qp] = r.val(stn)
                    else:
                        if li == nw:
                            r.wait("tensor", s_kx, c_kxp[p][0])
                            r.wait("tensor", s_kx2, c_kxp[p][1])
                        pos = li - nw
                        qeE = qeEL if pos < nlt else qeEG
                        qeO = qeOL if pos < nlt else qeOG
                        r.op("tensor", "matmul", X[:, 0:512],
                             kxE[:, pos * 128:(pos + 1) * 128], qeE[:],
                             start=True, stop=True)
                        r.opi("tensor", stn, 1, "matmul", X[:, 512:1024],
                              kxO[:, pos * 128:(pos + 1) * 128], qeO[:],
                              start=True, stop=True)
                    mm_done[g] = r.val(stn)
                    if li == len(kts) - 1:
                        kxc[0] = r.val(stn)

                for g in range(3):
                    emit_mm(g)
                for g in range(G):
                    p, li, kt = stream[g]
                    hp, kts, nw, nlt, nge = pair_meta[p]
                    h0, h1 = 2 * hp, 2 * hp + 1
                    s0 = float(SLOPES[h0])
                    s1 = float(SLOPES[h1])
                    n = len(kts)
                    if li == 0 and p + 1 < 8:
                        stage_qsl(p + 1)
                    if li == 0 and p in (3, 6):
                        dt_w = qc * 2 + (0 if p == 3 else 1)
                        r.opi("gpsimd", s_wout, 16, "dma_start",
                              woutb[:, dt_w * 1024:(dt_w + 1) * 1024],
                              wout[dt_w * 128:(dt_w + 1) * 128, :])
                    X = pss[g % 3]
                    pt = pts[g % 3]
                    if li < nw:
                        jn = 896 - 128 * (kt - qc * 8)
                        r.wait("vector", stn, mm_done[g])
                        r.op("vector", "scalar_tensor_tensor",
                             X[:, 0:512], dcs[:, jn:jn + 512], s0, X[:, 0:512],
                             AluOpType.mult, AluOpType.add)
                        r.opi("vector", svc, 1, "scalar_tensor_tensor",
                              X[:, 512:1024], dcs[:, jn:jn + 512], s1, X[:, 512:1024],
                              AluOpType.mult, AluOpType.add)
                        stt_done[g] = r.val(svc)
                        r.wait("scalar", svc, stt_done[g])
                    else:
                        r.wait("scalar", stn, mm_done[g])
                    if g >= 3:
                        r.wait("scalar", stn, pv_done[g - 3])
                    r.opi("scalar", ssc, 1, "activation", pt[:], X[:], AF.Exp, scale=0.125)
                    exp_done[g] = r.val(ssc)
                    r.wait("tensor", ssc, exp_done[g])
                    vs0 = vbuf[:, kt * (H * 65) + h0 * 65: kt * (H * 65) + h0 * 65 + 65]
                    vs1 = vbuf[:, kt * (H * 65) + h1 * 65: kt * (H * 65) + h1 * 65 + 65]
                    r.op("tensor", "matmul", pvtA[0:65, :], vs0, pt[:, 0:512],
                         start=(li == 0), stop=(li == n - 1))
                    r.opi("tensor", stn, 1, "matmul", pvtB[0:65, :], vs1, pt[:, 512:1024],
                          start=(li == 0), stop=(li == n - 1))
                    pv_done[g] = r.val(stn)
                    if li == n - 1:
                        r.wait("scalar", stn, pv_done[g])
                        r.wait("vector", stn, pv_done[g])
                        r.op("scalar", "copy", pvstack[0:64, hp * 512:(hp + 1) * 512], pvtA[0:64, :])
                        r.opi("scalar", ssc, 1, "copy", pvstack[64:65, SUM0 + hp * 512: SUM0 + (hp + 1) * 512], pvtA[64:65, :])
                        r.op("vector", "tensor_copy", pvstack[64:128, hp * 512:(hp + 1) * 512], pvtB[0:64, :])
                        r.opi("vector", svc, 1, "tensor_copy", pvstack[0:1, SUM0 + hp * 512: SUM0 + (hp + 1) * 512], pvtB[64:65, :])
                        stack_marks.append((r.val(ssc), r.val(svc)))
                    if g + 3 < G:
                        emit_mm(g + 3)
                c_stack = r.val(ssc)
                c_stackv = r.val(svc)

                attts = [attt, attt2]
                recs = [rec16, rec16b]
                s_atts = [s_att, s_att1]
                nrm_done = {}
                for qt in range(4):
                    tb = pvtAb if qt % 2 == 0 else pvtBb
                    sb = pvtBb[:, 0:32] if qt % 2 == 0 else pvtAb[:, 0:32]
                    if qt == 0:
                        r.wait("tensor", ssc, c_stack)
                        r.wait("tensor", svc, c_stackv)
                        r.wait("tensor", svc, r.val(svc))
                        r.wait("tensor", ssc, r.val(ssc))
                    elif qt >= 2:
                        r.wait("tensor", svc, nrm_done[qt - 2])
                    for j in range(8):
                        r.op("tensor", "transpose", tb[:, j * 128:(j + 1) * 128],
                             pvstack[:, j * 512 + qt * 128: j * 512 + qt * 128 + 128], idb[:])
                    if qt >= 1:
                        r.wait("tensor", svc, nrm_done[qt - 1])
                    for h in range(H):
                        rowp = 64 if h % 2 == 0 else 0
                        srcs = pvstack[rowp:rowp + 1,
                                       SUM0 + (h // 2) * 512 + qt * 128: SUM0 + (h // 2) * 512 + qt * 128 + 128]
                        r.op("tensor", "transpose", sb[:, 2 * h:2 * h + 1], srcs,
                             idb[rowp:rowp + 1, rowp:rowp + 1])
                    r.tag("tensor", stn, 1)
                    c_tp = r.val(stn)
                    r.wait("vector", stn, c_tp)
                    sums_v = sb[:, 0:32].rearrange("p (h two) -> p h two", two=2)[:, :, 0:1]
                    rc = recs[qt % 2]
                    r.op("vector", "reciprocal", rc[:].rearrange("p (h one) -> p h one", one=1), sums_v)
                    r.drain("vector")
                    for h in range(H):
                        r.op("vector", "tensor_scalar",
                             attq[:, h * 64:(h + 1) * 64],
                             tb[:, (h // 2) * 128 + (h % 2) * 64: (h // 2) * 128 + (h % 2) * 64 + 64],
                             rc[:, h:h + 1], None, AluOpType.mult)
                    r.tag("vector", svc, 1)
                    c_nrm = r.val(svc)
                    r.wait("tensor", svc, c_nrm)
                    for j in range(8):
                        r.op("tensor", "transpose", tb[:, j * 128:(j + 1) * 128],
                             attq[:, j * 128:(j + 1) * 128], idb[:])
                    r.tag("tensor", stn, 1)
                    c_tp2 = r.val(stn)
                    r.wait("vector", s_atts[qt % 2], r.val(s_atts[qt % 2]))
                    r.wait("vector", stn, c_tp2)
                    for j in range(8):
                        r.op("vector", "tensor_copy", attts[qt % 2][:, j * 128:(j + 1) * 128],
                             tb[:, j * 128:(j + 1) * 128])
                    r.tag("vector", svc, 1)
                    nrm_done[qt] = r.val(svc)
                    r.wait("gpsimd", svc, r.val(svc))
                    tti = qc * 4 + qt
                    r.opi("gpsimd", s_atts[qt % 2], 16, "dma_start",
                          at_hbm[:, tti * 1024:(tti + 1) * 1024], attts[qt % 2][:])
            c_woutd = r.val(s_wout)
            c_att_done = r.val(s_att)
            c_att_done1 = r.val(s_att1)
            c_att_tn = r.val(stn)
            r.flush(block)

            esatt.close()
            est = ExitStack()
            Ct = est.enter_context
            with est:
                w8t = Ct(nc.sbuf_tensor("w8t", [128, 8 * 1024], U8))
                wtmp0 = Ct(nc.sbuf_tensor("wtmp0", [128, 1024], F32))
                atile0 = Ct(nc.sbuf_tensor("atile0", [128, 1024], BF16))
                atile1 = Ct(nc.sbuf_tensor("atile1", [128, 1024], BF16))
                xot0 = Ct(nc.sbuf_tensor("xot0", [128, D], F32))
                xot1 = Ct(nc.sbuf_tensor("xot1", [128, D], F32))
                x1a = Ct(nc.sbuf_tensor("x1a", [128, D], F32))
                x1b = Ct(nc.sbuf_tensor("x1b", [128, D], F32))
                t1 = Ct(nc.sbuf_tensor("t1", [128, D], F32))
                xqb0 = Ct(nc.sbuf_tensor("xqb0", [128, D], BF16))
                xqb1 = Ct(nc.sbuf_tensor("xqb1", [128, D], BF16))
                x8c0 = Ct(nc.sbuf_tensor("x8c0", [128, D], U8))
                x8c1 = Ct(nc.sbuf_tensor("x8c1", [128, D], U8))
                outt0 = Ct(nc.sbuf_tensor("outt0", [128, D], F32))
                outt1 = Ct(nc.sbuf_tensor("outt1", [128, D], F32))
                lnst2 = Ct(nc.sbuf_tensor("lnst2", [128, 12], F32))
                lnmv2 = Ct(nc.sbuf_tensor("lnmv2", [128, 2], F32))
                lnsd2 = Ct(nc.sbuf_tensor("lnsd2", [128, 1], F32))
                lnstB = Ct(nc.sbuf_tensor("lnstB", [128, 12], F32))
                lnmvB = Ct(nc.sbuf_tensor("lnmvB", [128, 2], F32))
                lnsdB = Ct(nc.sbuf_tensor("lnsdB", [128, 1], F32))
                lnstC = Ct(nc.sbuf_tensor("lnstC", [128, 12], F32))
                lnmvC = Ct(nc.sbuf_tensor("lnmvC", [128, 2], F32))
                lnsdC = Ct(nc.sbuf_tensor("lnsdC", [128, 1], F32))
                t5 = Ct(nc.sbuf_tensor("t5", [128, D], F32))
                nb1 = Ct(nc.sbuf_tensor("nb1", [128, 1], F32))
                nb2 = Ct(nc.sbuf_tensor("nb2", [128, 1], F32))
                nb3 = Ct(nc.sbuf_tensor("nb3", [128, 1], F32))

                atiles = [atile0, atile1]
                s_atl = [s_atile, s_at1]
                xots = [xot0, xot1]
                x1s = [x1a, x1b]
                xqbs = [xqb0, xqb1]
                x8cs = [x8c0, x8c1]
                outts = [outt0, outt1]

                r.barrier([stn, svc, ssc, s_qsl, s_att, s_att1, s_dcs, s_wout])
                for it in range(8):
                    r.opi("sync", s_wtmp, 16, "dma_start", wtmp0[:], wfT[it * 128:(it + 1) * 128, :])
                    r.wait("vector", s_wtmp, r.val(s_wtmp))
                    r.opi("vector", svc, 1, "tensor_copy",
                          w8t[:, it * 1024:(it + 1) * 1024].bitcast(FP8), wtmp0[:])
                    r.wait("sync", svc, r.val(svc))
                c_w8 = r.val(svc)

                def ln_prep(src_ap):
                    r.op("vector", "bn_stats", lnst2[:, 0:6], src_ap[:, 0:512])
                    r.op("vector", "bn_stats", lnst2[:, 6:12], src_ap[:, 512:1024])
                    r.drain("vector")
                    r.op("vector", "bn_aggr", lnmv2[:], lnst2[:])
                    r.tag("vector", svc, 1)
                    r.wait("scalar", svc, r.val(svc))
                    r.op("scalar", "add", lnsd2[:], lnmv2[:, 1:2], EPS)
                    r.drain("scalar")
                    r.opi("scalar", ssc, 1, "activation", lnsd2[:], lnsd2[:], AF.Sqrt)
                    r.wait("vector", ssc, r.val(ssc))
                    r.op("vector", "reciprocal", lnsd2[:], lnsd2[:])
                    r.drain("vector")

                r.wait("sync", s_att, c_att_done)
                r.wait("sync", s_att1, c_att_done1)
                psW = [psA, psB]
                psWb = [psA[:].bitcast(BF16), psB[:].bitcast(BF16)]
                atile_cons = [0, 0]
                xot_free = [0, 0]
                mmW = [0, 0]
                t1act = [0, 0]
                aggrW = [0, 0]
                x1add = [0, 0]
                xqbact = [0, 0]
                transp = [0, 0]
                castm = [0, 0]
                mmF = [0, 0]
                aggrF = [0, 0]
                t1bact = [0, 0]
                c_x1add = [0]
                c_outadd = [0]
                load_marks = {}

                def psF_ap(pb, oc):
                    if pb == 0:
                        return psC[:, oc * 512:(oc + 1) * 512]
                    return pvtA[:] if oc == 0 else pvtB[:]

                def tail_load(tt):
                    pb = tt % 2
                    r.wait("sync", stn, atile_cons[pb])
                    r.opi("sync", s_atl[pb], 16, "dma_start",
                          atiles[pb][:], at_hbm[:, tt * 1024:(tt + 1) * 1024])
                    r.wait("sync", svc, xot_free[pb])
                    r.opi("sync", s_xot[pb], 16, "dma_start", xots[pb][:], xo[tt * 128:(tt + 1) * 128, :])
                    load_marks[tt] = (r.val(s_atl[pb]), r.val(s_xot[pb]))

                def S1(tt):
                    pb = tt % 2
                    r.wait("tensor", s_atl[pb], load_marks[tt][0])
                    if tt == 0:
                        r.wait("tensor", s_wout, c_woutd)
                    r.wait("tensor", svc, castm[pb])
                    r.wait("tensor", ssc, t1act[pb])
                    for oc in range(2):
                        for dt in range(8):
                            r.op("tensor", "matmul", psW[pb][:, oc * 512:(oc + 1) * 512],
                                 atiles[pb][:, dt * 128:(dt + 1) * 128],
                                 woutb[:, dt * 1024 + oc * 512: dt * 1024 + (oc + 1) * 512],
                                 start=(dt == 0), stop=False)
                        r.op("tensor", "matmul", psW[pb][:, oc * 512:(oc + 1) * 512],
                             onesb2[:], bout2[:, oc * 512:(oc + 1) * 512], start=False, stop=True)
                    r.tag("tensor", stn, 1)
                    atile_cons[pb] = r.val(stn)
                    mmW[pb] = r.val(stn)

                def S2(tt):
                    pb = tt % 2
                    r.wait("vector", stn, mmW[pb])
                    r.op("vector", "bn_stats", lnst2[:, 0:6], psW[pb][:, 0:512])
                    r.op("vector", "bn_stats", lnst2[:, 6:12], psW[pb][:, 512:1024])
                    r.drain("vector")
                    r.op("vector", "bn_aggr", lnmv2[:], lnst2[:])
                    r.tag("vector", svc, 1)
                    aggrW[pb] = r.val(svc)
                    r.wait("scalar", svc, aggrW[pb])
                    r.op("scalar", "add", lnsd2[:], lnmv2[:, 1:2], EPS)
                    r.drain("scalar")
                    r.opi("scalar", ssc, 1, "activation", lnsd2[:], lnsd2[:], AF.Sqrt)
                    r.wait("vector", ssc, r.val(ssc))
                    r.op("vector", "reciprocal", lnsd2[:], lnsd2[:])
                    r.drain("vector")
                    r.op("vector", "tensor_scalar", nb1[:], lnmv2[:, 0:1],
                         lnsd2[:, 0:1], -1.0, AluOpType.mult, AluOpType.mult)
                    r.tag("vector", svc, 1)
                    r.wait("scalar", svc, r.val(svc))
                    r.wait("scalar", svc, c_x1add[0])
                    r.opi("scalar", ssc, 1, "activation", t1[:], psW[pb][:], AF.Identity,
                          bias=nb1[:, 0:1], scale=lnsd2[:, 0:1])
                    t1act[pb] = r.val(ssc)
                    r.wait("vector", ssc, t1act[pb])
                    r.wait("vector", s_xot[pb], load_marks[tt][1])
                    r.opi("vector", svc, 1, "tensor_tensor", x1s[pb][:], t1[:], xots[pb][:], AluOpType.add)
                    xot_free[pb] = r.val(svc)
                    x1add[pb] = r.val(svc)
                    c_x1add[0] = r.val(svc)

                def S3(tt):
                    pb = tt % 2
                    r.drain("vector")
                    r.op("vector", "bn_stats", lnstB[:, 0:6], x1s[pb][:, 0:512])
                    r.op("vector", "bn_stats", lnstB[:, 6:12], x1s[pb][:, 512:1024])
                    r.drain("vector")
                    r.op("vector", "bn_aggr", lnmvB[:], lnstB[:])
                    r.tag("vector", svc, 1)
                    r.wait("scalar", svc, r.val(svc))
                    r.op("scalar", "add", lnsdB[:], lnmvB[:, 1:2], EPS)
                    r.drain("scalar")
                    r.opi("scalar", ssc, 1, "activation", lnsdB[:], lnsdB[:], AF.Sqrt)
                    r.wait("vector", ssc, r.val(ssc))
                    r.op("vector", "reciprocal", lnsdB[:], lnsdB[:])
                    r.drain("vector")
                    r.op("vector", "tensor_tensor", lnsdB[:], lnsdB[:], rin_b, AluOpType.mult)
                    r.drain("vector")
                    r.op("vector", "tensor_scalar", nb2[:], lnmvB[:, 0:1],
                         lnsdB[:, 0:1], -1.0, AluOpType.mult, AluOpType.mult)
                    r.tag("vector", svc, 1)
                    r.wait("scalar", svc, r.val(svc))
                    r.wait("scalar", stn, transp[pb])
                    r.opi("scalar", ssc, 1, "activation", xqbs[pb][:], x1s[pb][:], AF.Identity,
                          bias=nb2[:, 0:1], scale=lnsdB[:, 0:1])
                    xqbact[pb] = r.val(ssc)
                    r.wait("tensor", ssc, xqbact[pb])
                    r.wait("tensor", svc, aggrW[pb])
                    for j in range(8):
                        r.op("tensor", "transpose", psWb[pb][:, j * 128:(j + 1) * 128],
                             xqbs[pb][:, j * 128:(j + 1) * 128], idb[:])
                    r.tag("tensor", stn, 1)
                    transp[pb] = r.val(stn)
                    r.wait("vector", stn, transp[pb])
                    r.wait("vector", stn, mmF[pb])
                    r.opi("vector", svc, 1, "tensor_copy", x8cs[pb][:].bitcast(FP8),
                          psWb[pb][:, 0:1024])
                    castm[pb] = r.val(svc)

                def S4(tt):
                    pb = tt % 2
                    if tt == 0:
                        r.wait("tensor", svc, c_w8)
                    r.wait("tensor", svc, castm[pb])
                    r.wait("tensor", svc, aggrF[pb])
                    r.wait("tensor", ssc, t1bact[pb])
                    for oc in range(2):
                        dst = psF_ap(pb, oc)
                        for it in range(8):
                            r.op("tensor", "matmul", dst,
                                 x8cs[pb][:, it * 128:(it + 1) * 128].bitcast(FP8),
                                 w8t[:, it * 1024 + oc * 512: it * 1024 + (oc + 1) * 512].bitcast(FP8),
                                 start=(it == 0), stop=False)
                        r.op("tensor", "matmul", dst,
                             onesb2[:], bffn2[:, oc * 512:(oc + 1) * 512], start=False, stop=True)
                    r.tag("tensor", stn, 1)
                    mmF[pb] = r.val(stn)

                def S5(tt):
                    pb = tt % 2
                    r.wait("vector", stn, mmF[pb])
                    r.op("vector", "bn_stats", lnstC[:, 0:6], psF_ap(pb, 0))
                    r.op("vector", "bn_stats", lnstC[:, 6:12], psF_ap(pb, 1))
                    r.drain("vector")
                    r.op("vector", "bn_aggr", lnmvC[:], lnstC[:])
                    r.drain("vector")
                    r.op("vector", "tensor_scalar", lnsdC[:], lnmvC[:, 1:2],
                         ts2_b, EPS, AluOpType.mult, AluOpType.add)
                    r.tag("vector", svc, 1)
                    r.wait("scalar", svc, r.val(svc))
                    r.opi("scalar", ssc, 1, "activation", lnsdC[:], lnsdC[:], AF.Sqrt)
                    r.wait("vector", ssc, r.val(ssc))
                    r.op("vector", "reciprocal", lnsdC[:], lnsdC[:])
                    r.drain("vector")
                    r.op("vector", "tensor_tensor", lnsdC[:], lnsdC[:], ts_b, AluOpType.mult)
                    r.drain("vector")
                    r.op("vector", "tensor_scalar", nb3[:], lnmvC[:, 0:1],
                         lnsdC[:, 0:1], -1.0, AluOpType.mult, AluOpType.mult)
                    r.tag("vector", svc, 1)
                    aggrF[pb] = r.val(svc)
                    r.wait("scalar", svc, r.val(svc))
                    r.wait("scalar", svc, c_outadd[0])
                    r.opi("scalar", ssc, 1, "activation", t5[:, 0:512], psF_ap(pb, 0), AF.Identity,
                          bias=nb3[:, 0:1], scale=lnsdC[:, 0:1])
                    r.opi("scalar", ssc, 1, "activation", t5[:, 512:1024], psF_ap(pb, 1), AF.Identity,
                          bias=nb3[:, 0:1], scale=lnsdC[:, 0:1])
                    t1bact[pb] = r.val(ssc)
                    r.wait("vector", ssc, t1bact[pb])
                    r.wait("vector", s_out[pb], r.val(s_out[pb]))
                    r.opi("vector", svc, 1, "tensor_tensor", outts[pb][:], t5[:], x1s[pb][:], AluOpType.add)
                    c_outadd[0] = r.val(svc)
                    r.wait("sync", svc, c_outadd[0])
                    r.opi("sync", s_out[pb], 16, "dma_start", yout[tt * 128:(tt + 1) * 128, :], outts[pb][:])

                tail_load(0)
                tail_load(1)
                S1(0)
                S2(0)
                S1(1)
                for k in range(NT_OWN):
                    if k + 2 < NT_OWN:
                        tail_load(k + 2)
                    S3(k)
                    if k + 1 < NT_OWN:
                        S2(k + 1)
                    S4(k)
                    if k + 2 < NT_OWN:
                        S1(k + 2)
                    S5(k)
                r.wait("sync", s_out0, r.val(s_out0))
                r.wait("sync", s_out1, r.val(s_out1))
                r.flush(block)

    return nc


_NC_CACHE = None


def _get_nc():
    global _NC_CACHE
    if _NC_CACHE is None:
        _NC_CACHE = build_nc()
    return _NC_CACHE


def _host_inputs(L0_attn_x, L0_attn_Wqkv, L0_attn_bqkv, L0_attn_Wout, L0_attn_bout,
                 L0_ffn_weight_fp8, L0_ffn_bias, L0_ffn_input_scale, L0_ffn_weight_scale):
    X = np.ascontiguousarray(np.asarray(L0_attn_x, dtype=np.float32))
    Wqkv = np.ascontiguousarray(np.asarray(L0_attn_Wqkv, dtype=np.float32))
    bqkv = np.asarray(L0_attn_bqkv, dtype=np.float32)
    Wout = np.ascontiguousarray(np.asarray(L0_attn_Wout, dtype=np.float32))
    bout = np.ascontiguousarray(np.asarray(L0_attn_bout, dtype=np.float32).reshape(1, D))
    WfT = np.ascontiguousarray(np.asarray(L0_ffn_weight_fp8, dtype=np.float32).T)
    bffn = np.ascontiguousarray(np.asarray(L0_ffn_bias, dtype=np.float32).reshape(1, D))
    scl = np.array([[float(np.asarray(L0_ffn_input_scale).reshape(-1)[0]),
                     float(np.asarray(L0_ffn_weight_scale).reshape(-1)[0])]], dtype=np.float32)
    bkq = np.ascontiguousarray(bqkv[0:2 * D].reshape(16, 128).T)
    bv = np.ascontiguousarray(bqkv[2 * D:3 * D].reshape(1, D))
    idn = np.eye(128, dtype=ml_dtypes.bfloat16)

    dcores = []
    qrts = []
    for half in range(2):
        jj = np.arange(1408, dtype=np.int64)
        p = np.arange(128, dtype=np.int64)
        m = (jj - 896 + half * 512)[None, :] - p[:, None]
        dcores.append(np.ascontiguousarray((-8.0 * np.abs(m)).astype(np.float32)))
        t = np.arange(HALF, dtype=np.int64)
        qabs = (2 * (t // 512) + half) * 512 + (t % 512)
        qr = np.zeros((96, HALF), dtype=ml_dtypes.bfloat16)
        for h in range(H):
            v = -8.0 * float(SLOPES[h]) * (qabs.astype(np.float64) - 2048.0)
            for j, term in enumerate(_bf16_split3(v)):
                qr[3 * h + j] = term
            vg = -v
            for j, term in enumerate(_bf16_split3(vg)):
                qr[48 + 3 * h + j] = term
        qrts.append(np.ascontiguousarray(qr))
    k = np.arange(L, dtype=np.int64)
    u48v = np.zeros((48, L), dtype=ml_dtypes.bfloat16)
    for h in range(H):
        v = 8.0 * float(SLOPES[h]) * (k.astype(np.float64) - 2048.0)
        for j, term in enumerate(_bf16_split3(v)):
            u48v[3 * h + j] = term
    u48v = np.ascontiguousarray(u48v)
    ones_sgn = np.zeros((6, 3072), dtype=ml_dtypes.bfloat16)
    ones_sgn[0:3] = 1.0
    ones_sgn[3:6] = -1.0

    in_maps = []
    for c in range(8):
        b, half = c // 2, c % 2
        xo_c = np.ascontiguousarray(X[b].reshape(8, 512, D)[half::2].reshape(HALF, D))
        in_maps.append(dict(
            xf=X[b], xo=xo_c,
            wqkv=Wqkv, bkq=bkq, bv=bv, wout=Wout, bout=bout,
            wfT=WfT, bffn=bffn, scl=scl, dcore=dcores[half], idn=idn,
            u48=u48v, qrt=qrts[half], ones_sgn=ones_sgn))
    return in_maps


def kernel(**inputs):
    nc = _get_nc()
    in_maps = _host_inputs(**inputs)
    res = run_bass_kernel_spmd(nc, in_maps, core_ids=list(range(8)))
    out = np.empty((B, L, D), dtype=np.float32)
    for c in range(8):
        b, half = c // 2, c % 2
        out[b].reshape(8, 512, D)[half::2] = res.results[c]["yout"].reshape(4, 512, D)
    return out
